# version 11
# baseline (speedup 1.0000x reference)
import numpy as np
import concourse.bass as bass
import concourse.mybir as mybir
from concourse.bass_utils import run_bass_kernel_spmd

F32 = mybir.dt.float32
BF16 = mybir.dt.bfloat16
AF = mybir.ActivationFunctionType
ALU = mybir.AluOpType
AX = mybir.AxisListType

NCORES = 8
NSEQ = 2
T = 2048
D = 1024
CH = 128
NCH = T // CH
DC = D // 128
R = 512
O = 4096
E16 = 16
LR = 128

C_EPS = 0
C_I1024 = 1
C_NI1024 = 2
C_I16 = 3
C_HALF = 4
C_ONE = 5
C_NONE = 6
C_BSCALE = 7
C_ZERO = 8
C_NI16 = 9
C_LNBS = 10

ENGINES = ("sync", "tensor", "vector", "scalar", "gpsimd")


class Sem:
    __slots__ = ("name", "n", "handle")

    def __init__(self, name):
        self.name = name
        self.n = 0
        self.handle = None


class KB:

    def __init__(self):
        self.streams = {e: [] for e in ENGINES}
        self.sems = []

    def sem(self, name):
        s = Sem(name)
        self.sems.append(s)
        return s

    def em(self, eng, fn, inc=None):
        if inc is None:
            self.streams[eng].append(fn)
        else:
            sem, k = inc
            sem.n += k
            self.streams[eng].append(
                lambda e, fn=fn, sem=sem, k=k: fn(e).then_inc(sem.handle, k))

    def wait(self, eng, sem, val=None):
        v = sem.n if val is None else val
        assert v <= sem.n, f"wait on future count: {sem.name} {v} > {sem.n}"
        self.streams[eng].append(lambda e, sem=sem, v=v: e.wait_ge(sem.handle, v))


class Region:

    def __init__(self, ap_full, base, size, ap_bf=None):
        self.ap = ap_full
        self.ap_bf = ap_bf
        self.base = base
        self.off = base
        self.size = size

    def alloc(self, ncols, parts=128):
        a = self.ap
        o = self.off
        self.off += ncols
        assert self.off - self.base <= self.size, "arena region overflow"
        return a[0:parts, o:o + ncols]

    def alloc_bf(self, ncols_bf, parts=128):
        ncf = (ncols_bf + 1) // 2
        o = self.off
        self.off += ncf
        assert self.off - self.base <= self.size, "arena region overflow"
        return self.ap_bf[0:parts, 2 * o:2 * o + ncols_bf]


def build_program(n_passes=4):
    nc = bass.Bass("TRN2", target_bir_lowering=False)

    x_in = nc.dram_tensor("x_in", [NSEQ * T, D], F32, kind="ExternalInput").ap()
    resid = nc.dram_tensor("resid", [NSEQ * T, D], F32, kind="Internal").ap()
    out = nc.dram_tensor("out", [NSEQ * T, D], F32, kind="ExternalOutput").ap()

    d_wwr = nc.dram_tensor("wwr", [D, D], F32, kind="ExternalInput").ap()
    d_wrd = nc.dram_tensor("wrd", [D, D], F32, kind="ExternalInput").ap()
    d_scal = nc.dram_tensor("scal", [1, 2], F32, kind="ExternalInput").ap()

    d_wq = nc.dram_tensor("wq", [D, D], F32, kind="ExternalInput").ap()
    d_wk = nc.dram_tensor("wk", [D, D], F32, kind="ExternalInput").ap()
    d_wv = nc.dram_tensor("wv", [D, D], F32, kind="ExternalInput").ap()
    d_wo = nc.dram_tensor("wo", [D, D], F32, kind="ExternalInput").ap()
    d_wg1 = nc.dram_tensor("wg1", [D, LR], F32, kind="ExternalInput").ap()
    d_wg2 = nc.dram_tensor("wg2", [LR, D], F32, kind="ExternalInput").ap()

    d_proj = nc.dram_tensor("proj", [D, E16], F32, kind="ExternalInput").ap()
    d_c1c2 = nc.dram_tensor("c1c2", [E16, 2 * R], F32, kind="ExternalInput").ap()
    d_win1 = nc.dram_tensor("win1", [D, R], F32, kind="ExternalInput").ap()
    d_w1b = nc.dram_tensor("w1b", [1, R], F32, kind="ExternalInput").ap()
    d_wout1 = nc.dram_tensor("wout1", [R, O], F32, kind="ExternalInput").ap()
    d_win2 = nc.dram_tensor("win2", [O, R], F32, kind="ExternalInput").ap()
    d_w2b = nc.dram_tensor("w2b", [1, R], F32, kind="ExternalInput").ap()
    d_wout2 = nc.dram_tensor("wout2", [R, D], F32, kind="ExternalInput").ap()

    d_idn = nc.dram_tensor("idn", [128, 128], F32, kind="ExternalInput").ap()
    d_ccols = nc.dram_tensor("ccols", [128, 16], F32, kind="ExternalInput").ap()
    d_ones = nc.dram_tensor("onesr", [1, 128], F32, kind="ExternalInput").ap()
    d_diffa = nc.dram_tensor("diffa", [128, 128], F32, kind="ExternalInput").ap()
    d_cmw = nc.dram_tensor("cmw", [128, NCH], F32, kind="ExternalInput").ap()
    d_cmi = nc.dram_tensor("cmi", [128, NCH], F32, kind="ExternalInput").ap()
    d_tril = nc.dram_tensor("tril", [128, 128], F32, kind="ExternalInput").ap()
    d_trilb = nc.dram_tensor("trilb", [128, D], BF16, kind="ExternalInput").ap()
    d_u16 = nc.dram_tensor("u16", [E16, E16], F32, kind="ExternalInput").ap()
    d_e64c = nc.dram_tensor("e64c", [128, 192], F32, kind="ExternalInput").ap()
    d_w1h = nc.dram_tensor("w1h", [D, R], BF16, kind="Internal").ap()
    d_w1l = nc.dram_tensor("w1l", [D, R], BF16, kind="Internal").ap()
    d_w2h = nc.dram_tensor("w2h", [O, R], BF16, kind="Internal").ap()
    d_w2l = nc.dram_tensor("w2l", [O, R], BF16, kind="Internal").ap()
    d_wo1h = nc.dram_tensor("wo1h", [R, O], BF16, kind="Internal").ap()
    d_wo1l = nc.dram_tensor("wo1l", [R, O], BF16, kind="Internal").ap()
    d_wt2h = nc.dram_tensor("wt2h", [R, D], BF16, kind="Internal").ap()
    d_wt2l = nc.dram_tensor("wt2l", [R, D], BF16, kind="Internal").ap()
    d_pjh = nc.dram_tensor("pjh", [D, E16], BF16, kind="Internal").ap()
    d_pjl = nc.dram_tensor("pjl", [D, E16], BF16, kind="Internal").ap()
    d_qh = nc.dram_tensor("qh", [D, D], BF16, kind="Internal").ap()
    d_ql = nc.dram_tensor("ql", [D, D], BF16, kind="Internal").ap()
    d_kh = nc.dram_tensor("kh", [D, D], BF16, kind="Internal").ap()
    d_kl = nc.dram_tensor("kl", [D, D], BF16, kind="Internal").ap()
    d_vh = nc.dram_tensor("vh", [D, D], BF16, kind="Internal").ap()
    d_vl = nc.dram_tensor("vl", [D, D], BF16, kind="Internal").ap()
    d_oh = nc.dram_tensor("oh", [D, D], BF16, kind="Internal").ap()
    d_ol = nc.dram_tensor("ol", [D, D], BF16, kind="Internal").ap()
    d_g1hh = nc.dram_tensor("g1hh", [D, LR], BF16, kind="Internal").ap()
    d_g1ll = nc.dram_tensor("g1ll", [D, LR], BF16, kind="Internal").ap()
    d_g2hh = nc.dram_tensor("g2hh", [LR, D], BF16, kind="Internal").ap()
    d_g2ll = nc.dram_tensor("g2ll", [LR, D], BF16, kind="Internal").ap()
    d_wwrh = nc.dram_tensor("wwrh", [D, D], BF16, kind="Internal").ap()
    d_wwrl = nc.dram_tensor("wwrl", [D, D], BF16, kind="Internal").ap()
    d_wrdh = nc.dram_tensor("wrdh", [D, D], BF16, kind="Internal").ap()
    d_wrdl = nc.dram_tensor("wrdl", [D, D], BF16, kind="Internal").ap()

    B = KB()

    sLX = B.sem("sLX")
    sLW = B.sem("sLW")
    sSH = B.sem("sSH")
    sST = B.sem("sST")
    sT = B.sem("sT")
    sV = B.sem("sV")
    sS = B.sem("sS")
    sWO = B.sem("sWO")

    GLOBAL_COLS = 512
    ARENA_COLS = 52168

    with (
        nc.sbuf_tensor("glob", [128, GLOBAL_COLS], F32) as glob_t,
        nc.sbuf_tensor("arena", [128, ARENA_COLS], F32) as arena_t,
        nc.psum_tensor("ps0", [128, 512], F32) as ps0_t,
        nc.psum_tensor("ps1", [128, 512], F32) as ps1_t,
        nc.psum_tensor("pA", [128, 1024], F32) as pA_t,
        nc.psum_tensor("pB", [128, 1024], F32) as pB_t,
        nc.psum_tensor("pC", [128, 1024], F32) as pC_t,
        nc.Block() as block,
    ):
        glob = glob_t[:, :]
        arena = arena_t[:, :]
        arena_bf = arena_t.ap().bitcast(BF16)
        pT0 = ps0_t[:, :]
        pT1 = ps1_t[:, :]
        pA = pA_t[:, :]
        pB = pB_t[:, :]
        pC = pC_t[:, :]

        G = Region(glob, 0, GLOBAL_COLS)
        IDN = G.alloc(128)
        CC = G.alloc(16)
        ONESR = G.alloc(128, parts=1)
        SCAL = G.alloc(2, parts=1)
        LGC = G.alloc(1)
        ALC = G.alloc(1)
        SCW = G.alloc(NCH)
        SCI = G.alloc(NCH)
        MPA = G.alloc(128)

        def col(i):
            return CC[:, i:i + 1]

        def p(eng, fn, inc=None):
            B.em(eng, fn, inc)

        p("sync", lambda e: e.dma_start(out=IDN, in_=d_idn), (sLW, 16))
        p("sync", lambda e: e.dma_start(out=CC, in_=d_ccols), (sLW, 16))
        p("sync", lambda e: e.dma_start(out=ONESR, in_=d_ones), (sLW, 16))
        p("sync", lambda e: e.dma_start(out=SCAL, in_=d_scal), (sLW, 16))

        PR = Region(arena, 0, 512)
        DIFFA = PR.alloc(128)
        CMW = PR.alloc(NCH)
        CMI = PR.alloc(NCH)
        p("sync", lambda e: e.dma_start(out=DIFFA, in_=d_diffa), (sLW, 16))
        p("sync", lambda e: e.dma_start(out=CMW, in_=d_cmw), (sLW, 16))
        p("sync", lambda e: e.dma_start(out=CMI, in_=d_cmi), (sLW, 16))

        B.wait("scalar", sLW)
        sig1 = SCAL[0:1, 0:1]
        lg1 = ONESR
        TMP2 = G.alloc(4, parts=1)
        p("scalar", lambda e: e.activation(TMP2[0:1, 0:1], SCAL[0:1, 0:1], AF.Sigmoid),
          (sS, 1))
        B.wait("scalar", sS)
        p("scalar", lambda e: e.activation(TMP2[0:1, 1:2], TMP2[0:1, 0:1], AF.Ln),
          (sS, 1))
        B.wait("scalar", sS)
        p("scalar", lambda e: e.activation(TMP2[0:1, 2:3], SCAL[0:1, 1:2], AF.Exp),
          (sS, 1))
        B.wait("tensor", sS)
        B.wait("tensor", sLW)
        p("tensor", lambda e: e.matmul(pT0[:, 0:1], ONESR, TMP2[0:1, 1:2],
                                       start=True, stop=True))
        p("tensor", lambda e: e.matmul(pT0[:, 1:2], ONESR, TMP2[0:1, 2:3],
                                       start=True, stop=True), (sT, 1))
        B.wait("vector", sT)
        p("vector", lambda e: e.tensor_copy(LGC, pT0[:, 0:1]))
        p("vector", lambda e: e.tensor_copy(ALC, pT0[:, 1:2]), (sV, 1))
        B.wait("vector", sV)
        p("vector", lambda e: e.tensor_scalar_mul(SCW, CMW, LGC))
        p("vector", lambda e: e.tensor_scalar_mul(SCI, CMI, LGC))
        p("vector", lambda e: e.tensor_scalar_mul(MPA, DIFFA, LGC), (sV, 1))
        B.wait("scalar", sV)
        p("scalar", lambda e: e.activation(SCW, SCW, AF.Exp))
        p("scalar", lambda e: e.activation(SCI, SCI, AF.Exp))
        p("scalar", lambda e: e.activation(MPA, MPA, AF.Exp), (sS, 1))
        prologue_sS = sS.n

        def emit_pass_attn_a(src, dst):
            A = Region(arena, 0, ARENA_COLS, arena_bf)
            stage_f = [A.alloc(2048), A.alloc(2048)]
            stage_h = [A.alloc_bf(2048), A.alloc_bf(2048)]
            stage_l = [A.alloc_bf(2048), A.alloc_bf(2048)]
            B.wait("sync", sS, prologue_sS)
            split_sh = emit_split(
                [(d_wwr, D, D, d_wwrh, d_wwrl), (d_wrd, D, D, d_wrdh, d_wrdl)],
                stage_f, stage_h, stage_l)
            A.off = A.base
            WWRH = A.alloc_bf(DC * D)
            WWRL = A.alloc_bf(DC * D)
            WRDH = A.alloc_bf(DC * D)
            WRDL = A.alloc_bf(DC * D)
            UTHh = A.alloc_bf(D)
            UTLl = A.alloc_bf(D)
            RDTH = A.alloc_bf(D)
            RDTL = A.alloc_bf(D)
            WST = A.alloc(DC * D)
            XB = [A.alloc(D), A.alloc(D)]
            UBL = [A.alloc(D), A.alloc(D)]
            UTB = A.alloc(D)
            WKTB = A.alloc(D)
            WKBL = [A.alloc(D), A.alloc(D)]
            VB = A.alloc(D)
            VSB = A.alloc(D)
            SMB = A.alloc(128)
            RDS = A.alloc(D)
            LNSQ = A.alloc(D)
            FOUT0 = A.alloc(D)
            PREVR = A.alloc(D, parts=1)
            STASH = A.alloc(DC)
            STT = A.alloc(8)

            s1 = STT[:, 0:1]
            mean = STT[:, 1:2]
            var = STT[:, 2:3]
            rstd = STT[:, 3:4]
            ssq = STT[:, 5:6]
            sd = STT[:, 6:7]

            NTOT = NSEQ * NCH

            B.wait("sync", sSH, split_sh)
            for dstw, srcw in ((WWRH, d_wwrh), (WWRL, d_wwrl),
                               (WRDH, d_wrdh), (WRDL, d_wrdl)):
                p("sync", lambda e, dstw=dstw, srcw=srcw: e.dma_start(
                    out=dstw.rearrange("p (i e) -> p i e", i=DC),
                    in_=srcw.rearrange("(i p) e -> p i e", p=128)), (sLW, 16))
            w_ready = sLW.n

            ln_done = {}
            trans_done = {}
            shift_done = {-1: 0}
            wkt_dma = {-1: 0}
            rdt_evac = {-1: 0}
            wadd_done = {-1: 0}
            fin_done = {-1: 0}
            v_evac = {}
            wk_v = {}

            entry_sST = sST.n

            def emit_load(JJ):
                if JJ >= NTOT:
                    return
                r0 = JJ * CH
                xb = XB[JJ % 2]
                B.wait("sync", sV, fin_done.get(JJ - 2, 0))
                B.wait("sync", sST, entry_sST)
                p("sync", lambda e, r0=r0, xb=xb: e.dma_start(
                    out=xb, in_=src[r0:r0 + CH, :]), (sLX, 16))
                return sLX.n

            def emit_ln(JJ):
                if JJ >= NTOT:
                    return
                xb = XB[JJ % 2]
                ub = UBL[JJ % 2]
                B.wait("vector", sLX)
                p("vector", lambda e, xb=xb: e.reduce_sum(s1, xb, axis=AX.X), (sV, 1))
                B.wait("scalar", sLX)
                p("scalar", lambda e, xb=xb: e.activation(
                    LNSQ, xb, AF.Square, accum_out=ssq), (sS, 1))
                B.wait("vector", sV)
                B.wait("vector", sS)
                p("vector", lambda e: e.tensor_scalar_mul(mean, s1, col(C_NI1024)))
                p("vector", lambda e: e.tensor_scalar_mul(var, ssq, col(C_I1024)),
                  (sV, 1))
                B.wait("vector", sV)
                p("vector", lambda e: e.tensor_tensor(sd, mean, mean, op=ALU.mult),
                  (sV, 1))
                B.wait("vector", sV)
                p("vector", lambda e: e.tensor_tensor(var, var, sd, op=ALU.subtract),
                  (sV, 1))
                B.wait("scalar", sV)
                p("scalar", lambda e: e.activation(sd, var, AF.Sqrt, bias=col(C_EPS)),
                  (sS, 1))
                B.wait("vector", sS)
                p("vector", lambda e: e.reciprocal(rstd, sd), (sV, 1))
                B.wait("vector", sV)
                B.wait("vector", sT, trans_done.get(JJ - 2, 0))
                B.wait("vector", sSH, shift_done.get(JJ - 2, 0))
                p("vector", lambda e, xb=xb, ub=ub: e.tensor_scalar(
                    ub, xb, mean, rstd, op0=ALU.add, op1=ALU.mult), (sV, 1))
                ln_done[JJ] = sV.n

            def emit_shift(JJ):
                if JJ >= NTOT:
                    return
                Jloc = JJ % NCH
                ub = UBL[JJ % 2]
                wkb = WKBL[JJ % 2]
                B.wait("sync", sV, ln_done[JJ])
                p("sync", lambda e, ub=ub, wkb=wkb: e.dma_start(
                    out=wkb[1:128, :], in_=ub[0:127, :]), (sSH, 16))
                if Jloc == 0:
                    p("vector", lambda e, wkb=wkb: e.memset(wkb[0:1, :], 0.0), (sV, 1))
                else:
                    p("sync", lambda e, wkb=wkb: e.dma_start(out=wkb[0:1, :], in_=PREVR),
                      (sSH, 16))
                wk_sh = sSH.n
                wk_v[JJ] = sV.n
                B.wait("sync", sSH, wk_sh)
                p("sync", lambda e, ub=ub: e.dma_start(out=PREVR, in_=ub[127:128, :]),
                  (sSH, 16))
                shift_done[JJ] = sSH.n

            emit_load(0)
            emit_ln(0)
            emit_shift(0)
            emit_load(1)

            for JJ in range(NTOT):
                Jloc = JJ % NCH
                r0 = JJ * CH
                xb = XB[JJ % 2]
                ub = UBL[JJ % 2]
                wkb = WKBL[JJ % 2]

                B.wait("tensor", sV, ln_done[JJ])
                B.wait("tensor", sV, rdt_evac.get(JJ - 1, 0))
                B.wait("tensor", sSH, wkt_dma.get(JJ - 1, 0))
                for h in range(2):
                    pdst = (pT0 if h == 0 else pT1)
                    for i in range(4):
                        ii = h * 4 + i
                        p("tensor", lambda e, ii=ii, pdst=pdst, i=i, ub=ub: e.transpose(
                            pdst[:, i * 128:(i + 1) * 128],
                            ub[:, ii * 128:(ii + 1) * 128], IDN),
                          (sT, 1) if i == 3 else None)
                    B.wait("vector", sT)
                    p("vector", lambda e, h=h, pdst=pdst: e.tensor_copy(
                        UTB[:, h * 512:(h + 1) * 512], pdst))
                    p("vector", lambda e, h=h, pdst=pdst: e.tensor_copy(
                        UTHh[:, h * 512:(h + 1) * 512], pdst), (sV, 1))
                    B.wait("vector", sV)
                    p("vector", lambda e, h=h, pdst=pdst: e.tensor_tensor(
                        UTLl[:, h * 512:(h + 1) * 512], pdst,
                        UTHh[:, h * 512:(h + 1) * 512], op=ALU.subtract), (sV, 1))
                trans_done[JJ] = sT.n
                ut_done = sV.n

                B.wait("sync", sV, ut_done)
                p("sync", lambda e: e.dma_start(
                    out=WKTB.rearrange("p (i c) -> p i c", i=DC)[:, :, 1:128],
                    in_=UTB.rearrange("p (i c) -> p i c", i=DC)[:, :, 0:127]),
                  (sSH, 16))
                wkt_dma[JJ] = sSH.n
                if Jloc == 0:
                    p("vector", lambda e: e.memset(STASH, 0.0), (sV, 1))
                    B.wait("vector", sV)
                B.wait("vector", sSH, wkt_dma[JJ])
                p("vector", lambda e: e.tensor_copy(WKTB[:, 0:D:128], STASH),
                  (sV, 1))
                B.wait("vector", sV)
                p("vector", lambda e: e.tensor_copy(STASH, UTB[:, 127:D:128]),
                  (sV, 1))
                wkt_done = sV.n

                B.wait("tensor", sLW, w_ready)
                B.wait("tensor", sV, ut_done)
                for h in range(2):
                    for i in range(DC):
                        mm3(pA[:, h * 512:(h + 1) * 512],
                            UTHh[:, i * 128:(i + 1) * 128],
                            UTLl[:, i * 128:(i + 1) * 128],
                            WWRH[:, i * D + h * 512:i * D + (h + 1) * 512],
                            WWRL[:, i * D + h * 512:i * D + (h + 1) * 512],
                            start=(i == 0), stop=(i == DC - 1),
                            inc=(sT, 1) if (i == DC - 1 and h == 1) else None)
                B.wait("vector", sT)
                p("vector", lambda e: e.tensor_copy(VB, pA))
                p("vector", lambda e, Jloc=Jloc: e.tensor_scalar_mul(
                    VSB, pA, SCW[:, Jloc:Jloc + 1]), (sV, 1))
                v_evac[JJ] = sV.n

                B.wait("tensor", sV, wkt_done)
                for i in range(DC):
                    p("tensor", lambda e, i=i: e.matmul(
                        pT1[:, 0:128],
                        WKTB[:, i * 128:(i + 1) * 128],
                        UTB[:, i * 128:(i + 1) * 128],
                        start=(i == 0), stop=(i == DC - 1)),
                      (sT, 1) if i == DC - 1 else None)
                B.wait("vector", sT)
                p("vector", lambda e: e.tensor_tensor(
                    SMB, pT1[:, 0:128], MPA, op=ALU.mult), (sV, 1))
                sm_done = sV.n

                if Jloc > 0:
                    B.wait("tensor", sV, wadd_done[JJ - 1])
                    for h in range(2):
                        for i in range(DC):
                            p("tensor", lambda e, h=h, i=i: e.matmul(
                                pB[:, h * 512:(h + 1) * 512],
                                UTB[:, i * 128:(i + 1) * 128],
                                WST[:, i * D + h * 512:i * D + (h + 1) * 512],
                                start=(i == 0), stop=(i == DC - 1)))
                B.wait("tensor", sV, sm_done)
                for h in range(2):
                    p("tensor", lambda e, h=h: e.matmul(
                        pC[:, h * 512:(h + 1) * 512], SMB,
                        VB[:, h * 512:(h + 1) * 512],
                        start=True, stop=True),
                      (sT, 1) if h == 1 else None)
                B.wait("vector", sT)
                if Jloc > 0:
                    p("vector", lambda e, Jloc=Jloc: e.tensor_scalar_mul(
                        RDS, pB, SCI[:, Jloc:Jloc + 1]))
                    p("vector", lambda e: e.tensor_tensor(
                        RDS, RDS, pC, op=ALU.add), (sV, 1))
                else:
                    p("vector", lambda e: e.tensor_copy(RDS, pC), (sV, 1))
                rds_done = sV.n

                B.wait("tensor", sV, rds_done)
                for h in range(2):
                    pdst = (pT0 if h == 0 else pT1)
                    for i in range(4):
                        ii = h * 4 + i
                        p("tensor", lambda e, ii=ii, pdst=pdst, i=i: e.transpose(
                            pdst[:, i * 128:(i + 1) * 128],
                            RDS[:, ii * 128:(ii + 1) * 128], IDN),
                          (sT, 1) if i == 3 else None)
                    B.wait("vector", sT)
                    p("vector", lambda e, h=h, pdst=pdst: e.tensor_copy(
                        RDTH[:, h * 512:(h + 1) * 512], pdst), (sV, 1))
                    B.wait("vector", sV)
                    p("vector", lambda e, h=h, pdst=pdst: e.tensor_tensor(
                        RDTL[:, h * 512:(h + 1) * 512], pdst,
                        RDTH[:, h * 512:(h + 1) * 512], op=ALU.subtract), (sV, 1))
                rdt_evac[JJ] = sV.n

                B.wait("tensor", sV, rdt_evac[JJ])
                for h in range(2):
                    for i in range(DC):
                        mm3(pA[:, h * 512:(h + 1) * 512],
                            RDTH[:, i * 128:(i + 1) * 128],
                            RDTL[:, i * 128:(i + 1) * 128],
                            WRDH[:, i * D + h * 512:i * D + (h + 1) * 512],
                            WRDL[:, i * D + h * 512:i * D + (h + 1) * 512],
                            start=(i == 0), stop=(i == DC - 1),
                            inc=(sT, 1) if (i == DC - 1 and h == 1) else None)
                ro_done = sT.n

                emit_ln(JJ + 1)
                emit_shift(JJ + 1)

                B.wait("tensor", sSH, shift_done[JJ])
                B.wait("tensor", sV, max(wk_v.get(JJ, 0), v_evac[JJ], rds_done))
                wadd_base = None
                for j in range(DC):
                    pd = pB if (j % 2 == 0) else pC
                    if j >= 2:
                        B.wait("tensor", sV, wadd_base + (j - 2) + 1)
                    for h in range(2):
                        p("tensor", lambda e, j=j, h=h, pd=pd, wkb=wkb: e.matmul(
                            pd[:, h * 512:(h + 1) * 512],
                            wkb[:, j * 128:(j + 1) * 128],
                            VSB[:, h * 512:(h + 1) * 512],
                            start=True, stop=True),
                          (sT, 1) if h == 1 else None)
                    if j == 0:
                        wadd_base = sV.n
                    B.wait("vector", sT)
                    if Jloc == 0:
                        p("vector", lambda e, j=j, pd=pd: e.tensor_copy(
                            WST[:, j * D:(j + 1) * D], pd), (sV, 1))
                    else:
                        p("vector", lambda e, j=j, pd=pd: e.tensor_tensor(
                            WST[:, j * D:(j + 1) * D],
                            WST[:, j * D:(j + 1) * D], pd, op=ALU.add), (sV, 1))
                wadd_done[JJ] = sV.n

                B.wait("vector", sT, ro_done)
                B.wait("vector", sST)
                p("vector", lambda e, ub=ub: e.scalar_tensor_tensor(
                    FOUT0, pA, ALC, ub, op0=ALU.mult, op1=ALU.add))
                p("vector", lambda e, xb=xb: e.tensor_tensor(
                    FOUT0, FOUT0, xb, op=ALU.add), (sV, 1))
                fin_done[JJ] = sV.n

                B.wait("gpsimd", sV, fin_done[JJ])
                p("gpsimd", lambda e, r0=r0, dst=dst: e.dma_start(
                    out=dst[r0:r0 + CH, :], in_=FOUT0), (sST, 16))

                emit_load(JJ + 2)

        GRP = 4
        NGRP = NCH // GRP

        def mm3(pz, sh, sl, mh, ml, start, stop, inc=None):
            p("tensor", lambda e: e.matmul(pz, sh, mh, start=start, stop=False))
            p("tensor", lambda e: e.matmul(pz, sh, ml, start=False, stop=False))
            B.em("tensor", lambda e: e.matmul(pz, sl, mh, start=False, stop=stop),
                 inc)

        def emit_split(jobs, stage_f, stage_h, stage_l):
            cast_done = {}
            store_done = {}
            i = 0
            for dw, Rw, Cw, dh, dl in jobs:
                csplit = 1 if Cw <= 2048 else Cw // 2048
                cw = Cw // csplit
                g = max(1, min(Rw // 128, 2048 // cw))
                w3 = dw.rearrange("(n p) c -> p n c", p=128)
                h3 = dh.rearrange("(n p) c -> p n c", p=128)
                l3 = dl.rearrange("(n p) c -> p n c", p=128)
                for cs in range(csplit):
                    for k0 in range(0, Rw // 128, g):
                        q = i % 2
                        W = g * cw
                        B.wait("sync", sV, cast_done.get(i - 2, 0))
                        p("sync", lambda e, q=q, W=W, k0=k0, g=g, cs=cs, cw=cw,
                          w3=w3: e.dma_start(
                              out=stage_f[q][:, 0:W].rearrange(
                                  "p (n c) -> p n c", n=g),
                              in_=w3[:, k0:k0 + g, cs * cw:(cs + 1) * cw]),
                          (sLW, 16))
                        B.wait("vector", sLW)
                        B.wait("vector", sSH, store_done.get(i - 2, 0))
                        p("vector", lambda e, q=q, W=W: e.tensor_copy(
                            stage_h[q][:, 0:W], stage_f[q][:, 0:W]), (sV, 1))
                        B.wait("vector", sV)
                        p("vector", lambda e, q=q, W=W: e.tensor_tensor(
                            stage_l[q][:, 0:W], stage_f[q][:, 0:W],
                            stage_h[q][:, 0:W], op=ALU.subtract), (sV, 1))
                        cast_done[i] = sV.n
                        B.wait("sync", sV, cast_done[i])
                        for st, d3 in ((stage_h, h3), (stage_l, l3)):
                            p("sync", lambda e, q=q, W=W, k0=k0, g=g, cs=cs,
                              cw=cw, st=st, d3=d3: e.dma_start(
                                  out=d3[:, k0:k0 + g, cs * cw:(cs + 1) * cw],
                                  in_=st[q][:, 0:W].rearrange(
                                      "p (n c) -> p n c", n=g)),
                              (sSH, 16))
                        store_done[i] = sSH.n
                        i += 1
            return sSH.n

        ffn_split_state = {}

        def emit_pass_ffn(src, dst):
            A = Region(arena, 0, ARENA_COLS, arena_bf)
            if "done" not in ffn_split_state:
                stage_f = [A.alloc(2048), A.alloc(2048)]
                stage_h = [A.alloc_bf(2048), A.alloc_bf(2048)]
                stage_l = [A.alloc_bf(2048), A.alloc_bf(2048)]
                B.wait("sync", sV)
                B.wait("sync", sT)
                B.wait("sync", sS)
                B.wait("vector", sST)
                B.wait("scalar", sST)
                sh_done = emit_split(
                    [(d_proj, D, E16, d_pjh, d_pjl),
                     (d_win1, D, R, d_w1h, d_w1l),
                     (d_wout1, R, O, d_wo1h, d_wo1l),
                     (d_win2, O, R, d_w2h, d_w2l),
                     (d_wout2, R, D, d_wt2h, d_wt2l)],
                    stage_f, stage_h, stage_l)
                ffn_split_state["done"] = sh_done
                A.off = A.base
            split_sh = ffn_split_state["done"]

            WIN1H = A.alloc_bf(DC * R)
            WIN1L = A.alloc_bf(DC * R)
            WOUT2H = A.alloc_bf(4 * D)
            WOUT2L = A.alloc_bf(4 * D)
            PROJH = A.alloc_bf(DC * E16)
            PROJL = A.alloc_bf(DC * E16)
            UTH = A.alloc_bf(D)
            UTL = A.alloc_bf(D)
            Z1TH = A.alloc_bf(4 * 512)
            Z1TL = A.alloc_bf(4 * 512)
            WO1BH = [A.alloc_bf(4 * 128) for _ in range(3)]
            WO1BL = [A.alloc_bf(4 * 128) for _ in range(3)]
            W2SH = [A.alloc_bf(R) for _ in range(3)]
            W2SL = [A.alloc_bf(R) for _ in range(3)]
            HTH = [A.alloc_bf(R), A.alloc_bf(R)]
            HTL = [A.alloc_bf(R), A.alloc_bf(R)]
            Z2TH = A.alloc_bf(R)
            Z2TL = A.alloc_bf(R)
            C12 = A.alloc(2 * R, parts=16)
            W1B = A.alloc(R, parts=1)
            W2B = A.alloc(R, parts=1)
            U16S = A.alloc(E16, parts=16)
            E64C = A.alloc(3 * 64)
            XGL = [[A.alloc(D) for _ in range(GRP)], [A.alloc(D) for _ in range(GRP)]]
            UB4 = [A.alloc(D) for _ in range(GRP)]
            LNSQF = A.alloc(D)
            Z1F = [A.alloc(R) for _ in range(GRP)]
            PRAW = A.alloc(64)
            CEN = A.alloc(64)
            PALL = A.alloc(64)
            SRT = [A.alloc(64), A.alloc(64)]
            SQ64 = A.alloc(64)
            CS64 = A.alloc(64)
            CSQ64 = A.alloc(64)
            MEAN64 = A.alloc(64)
            WRK64 = A.alloc(64)
            TAU64 = A.alloc(64)
            AOUT = A.alloc(64)
            ST4 = A.alloc(16)
            ATG = A.alloc(GRP * 128, parts=16)
            ZST = A.alloc(128, parts=16)
            ZSQT = A.alloc(128, parts=16)
            HTB = [A.alloc(R), A.alloc(R)]
            GB = A.alloc(R)
            Z2S = A.alloc(R)
            FOUT = A.alloc(D)

            nm4 = ST4[:, 0:4]
            v4 = ST4[:, 4:8]
            sd4 = ST4[:, 8:12]
            supp4 = ST4[:, 12:16]
            IOTA64 = E64C[:, 0:64]
            KINV64 = E64C[:, 64:128]
            KVAL64 = E64C[:, 128:192]

            def seg(ap64):
                return ap64.rearrange("p (s e) -> p s e", e=E16)

            def bc(ap64seg, ap4):
                return bass.broadcast_tensor_aps(
                    ap64seg, ap4.rearrange("p (s one) -> p s one", one=1))

            B.wait("sync", sV)
            B.wait("sync", sT)
            B.wait("sync", sS)
            B.wait("sync", sSH, split_sh)
            B.wait("vector", sST)
            B.wait("scalar", sST)
            for dstw, srcw in ((WIN1H, d_w1h), (WIN1L, d_w1l),
                               (WOUT2H, d_wt2h), (WOUT2L, d_wt2l),
                               (PROJH, d_pjh), (PROJL, d_pjl)):
                nrows = srcw.shape[0] // 128
                p("sync", lambda e, dstw=dstw, srcw=srcw, nrows=nrows: e.dma_start(
                    out=dstw.rearrange("p (i r) -> p i r", i=nrows),
                    in_=srcw.rearrange("(i p) r -> p i r", p=128)), (sLW, 16))
            p("sync", lambda e: e.dma_start(out=C12, in_=d_c1c2), (sLW, 16))
            p("sync", lambda e: e.dma_start(out=W1B, in_=d_w1b), (sLW, 16))
            p("sync", lambda e: e.dma_start(out=W2B, in_=d_w2b), (sLW, 16))
            p("sync", lambda e: e.dma_start(out=U16S, in_=d_u16), (sLW, 16))
            p("sync", lambda e: e.dma_start(out=E64C, in_=d_e64c), (sLW, 16))
            w_ready = sLW.n
            entry_sST = sST.n

            NGRPT = NSEQ * NGRP
            ln_done = {}
            fin_done = {}
            trans_done = {}
            prev_ht_tail = [0]

            def vg(fn):
                p("vector", fn, (sV, 1))
                B.wait("vector", sV)

            def emit_lnblock(gg):
                if gg >= NGRPT:
                    return
                xg = XGL[gg % 2]
                for c in range(GRP):
                    JJ = gg * GRP + c
                    r0 = JJ * CH
                    B.wait("gpsimd", sV, fin_done.get((gg - 2, c), 0))
                    p("gpsimd", lambda e, r0=r0, xb=xg[c]: e.dma_start(
                        out=xb, in_=src[r0:r0 + CH, :]), (sLX, 16))
                B.wait("vector", sLX)
                B.wait("scalar", sLX)
                B.wait("vector", sT, trans_done.get(gg - 1, 0))
                for c in range(GRP):
                    p("vector", lambda e, c=c, xg=xg: e.reduce_sum(
                        nm4[:, c:c + 1], xg[c], axis=AX.X),
                      (sV, 1) if c == GRP - 1 else None)
                    p("scalar", lambda e, c=c, xg=xg: e.activation(
                        LNSQF, xg[c], AF.Square, accum_out=v4[:, c:c + 1]),
                      (sS, 1) if c == GRP - 1 else None)
                B.wait("vector", sV)
                B.wait("vector", sS)
                vg(lambda e: e.tensor_scalar_mul(nm4, nm4, col(C_NI1024)))
                vg(lambda e: e.tensor_scalar_mul(v4, v4, col(C_I1024)))
                vg(lambda e: e.tensor_tensor(sd4, nm4, nm4, op=ALU.mult))
                vg(lambda e: e.tensor_tensor(v4, v4, sd4, op=ALU.subtract))
                B.wait("scalar", sV)
                p("scalar", lambda e: e.activation(sd4, v4, AF.Sqrt, bias=col(C_EPS)),
                  (sS, 1))
                B.wait("vector", sS)
                vg(lambda e: e.reciprocal(sd4, sd4))
                for c in range(GRP):
                    p("vector", lambda e, c=c, xg=xg: e.tensor_scalar(
                        UB4[c], xg[c], nm4[:, c:c + 1], sd4[:, c:c + 1],
                        op0=ALU.add, op1=ALU.mult),
                      (sV, 1) if c == GRP - 1 else None)
                ln_done[gg] = sV.n

            emit_lnblock(0)

            for gg in range(NGRPT):
                g = gg % NGRP
                xg = XGL[gg % 2]

                B.wait("tensor", sV, ln_done[gg])
                B.wait("tensor", sV)
                B.wait("tensor", sS)
                B.wait("tensor", sLW, w_ready)
                for c in range(GRP):
                    for h in range(2):
                        pdst = (pT0 if h == 0 else pT1)
                        for i in range(4):
                            ii = h * 4 + i
                            p("tensor", lambda e, ii=ii, pdst=pdst, i=i, c=c:
                              e.transpose(pdst[:, i * 128:(i + 1) * 128],
                                          UB4[c][:, ii * 128:(ii + 1) * 128], IDN),
                              (sT, 1) if i == 3 else None)
                        B.wait("vector", sT)
                        p("vector", lambda e, h=h, pdst=pdst: e.tensor_copy(
                            UTH[:, h * 512:(h + 1) * 512], pdst), (sV, 1))
                        B.wait("vector", sV)
                        p("vector", lambda e, h=h, pdst=pdst: e.tensor_tensor(
                            UTL[:, h * 512:(h + 1) * 512], pdst,
                            UTH[:, h * 512:(h + 1) * 512], op=ALU.subtract),
                          (sV, 1))
                    ut_done = sV.n
                    B.wait("tensor", sV, ut_done)
                    for i in range(DC):
                        mm3(pA[:, 0:16],
                            UTH[:, i * 128:(i + 1) * 128],
                            UTL[:, i * 128:(i + 1) * 128],
                            PROJH[:, i * E16:(i + 1) * E16],
                            PROJL[:, i * E16:(i + 1) * E16],
                            start=(i == 0), stop=(i == DC - 1))
                    for i in range(DC):
                        mm3(pA[:, 512:1024],
                            UTH[:, i * 128:(i + 1) * 128],
                            UTL[:, i * 128:(i + 1) * 128],
                            WIN1H[:, i * R:(i + 1) * R],
                            WIN1L[:, i * R:(i + 1) * R],
                            start=(i == 0), stop=False)
                    p("tensor", lambda e: e.matmul(
                        pA[:, 512:1024], ONESR, W1B, start=False, stop=True),
                      (sT, 1))
                    B.wait("vector", sT)
                    p("vector", lambda e, c=c: e.tensor_copy(Z1F[c], pA[:, 512:1024]))
                    p("vector", lambda e, c=c: e.tensor_copy(
                        PRAW[:, c * E16:(c + 1) * E16], pA[:, 0:16]), (sV, 1))
                trans_done[gg] = sT.n
                praw_done = sV.n

                B.wait("vector", sV, praw_done)
                vg(lambda e: e.reduce_sum(nm4, seg(PRAW), axis=AX.X))
                vg(lambda e: e.tensor_scalar_mul(nm4, nm4, col(C_NI16)))
                _pa, _pb = bc(seg(PRAW), nm4)
                p("vector", lambda e, _pa=_pa, _pb=_pb: e.tensor_tensor(
                    seg(CEN), _pa, _pb, op=ALU.add), (sV, 1))
                B.wait("vector", sV)
                vg(lambda e: e.tensor_tensor(WRK64, CEN, CEN, op=ALU.mult))
                vg(lambda e: e.reduce_sum(v4, seg(WRK64), axis=AX.X))
                vg(lambda e: e.tensor_scalar_mul(v4, v4, col(C_I16)))
                B.wait("scalar", sV)
                p("scalar", lambda e: e.activation(sd4, v4, AF.Sqrt, bias=col(C_EPS)),
                  (sS, 1))
                B.wait("vector", sS)
                vg(lambda e: e.reciprocal(sd4, sd4))
                vg(lambda e: e.tensor_scalar_mul(sd4, sd4, col(C_HALF)))
                _ca, _cb = bc(seg(CEN), sd4)
                p("vector", lambda e, _ca=_ca, _cb=_cb: e.tensor_tensor(
                    seg(PALL), _ca, _cb, op=ALU.mult), (sV, 1))
                B.wait("vector", sV)
                vg(lambda e: e.reduce_max(supp4, seg(PALL), axis=AX.X))
                _za, _zb = bc(seg(PALL), supp4)
                p("vector", lambda e, _za=_za, _zb=_zb: e.tensor_tensor(
                    seg(PALL), _za, _zb, op=ALU.subtract), (sV, 1))
                B.wait("vector", sV)
                cur, nxt = 0, 1
                vg(lambda e: e.tensor_copy(SRT[0], PALL))
                for rnd in range(16):
                    a3 = SRT[cur].rearrange("p (s e) -> p s e", e=E16)
                    b3 = SRT[nxt].rearrange("p (s e) -> p s e", e=E16)
                    if rnd % 2 == 0:
                        av = a3.rearrange("p s (q two) -> p s q two", two=2)
                        bv = b3.rearrange("p s (q two) -> p s q two", two=2)
                        p("vector", lambda e, av=av, bv=bv: e.tensor_tensor(
                            bv[:, :, :, 0], av[:, :, :, 0], av[:, :, :, 1],
                            op=ALU.max))
                        p("vector", lambda e, av=av, bv=bv: e.tensor_tensor(
                            bv[:, :, :, 1], av[:, :, :, 0], av[:, :, :, 1],
                            op=ALU.min), (sV, 1))
                        B.wait("vector", sV)
                    else:
                        am = a3[:, :, 1:15].rearrange("p s (q two) -> p s q two",
                                                      two=2)
                        bm = b3[:, :, 1:15].rearrange("p s (q two) -> p s q two",
                                                      two=2)
                        p("vector", lambda e, am=am, bm=bm: e.tensor_tensor(
                            bm[:, :, :, 0], am[:, :, :, 0], am[:, :, :, 1],
                            op=ALU.max))
                        p("vector", lambda e, am=am, bm=bm: e.tensor_tensor(
                            bm[:, :, :, 1], am[:, :, :, 0], am[:, :, :, 1],
                            op=ALU.min))
                        p("vector", lambda e, a3=a3, b3=b3: e.tensor_copy(
                            b3[:, :, 0:1], a3[:, :, 0:1]))
                        p("vector", lambda e, a3=a3, b3=b3: e.tensor_copy(
                            b3[:, :, 15:16], a3[:, :, 15:16]), (sV, 1))
                        B.wait("vector", sV)
                    cur, nxt = nxt, cur
                ZS = SRT[cur]
                B.wait("scalar", sV)
                p("scalar", lambda e: e.activation(SQ64, ZS, AF.Square), (sS, 1))
                sq_done = sS.n

                for c in range(GRP):
                    zc = ZS[:, c * E16:(c + 1) * E16]
                    B.wait("tensor", sV)
                    B.wait("tensor", sS, sq_done)
                    p("tensor", lambda e, zc=zc: e.transpose(
                        pT0[0:16, 0:128], zc, IDN))
                    p("tensor", lambda e, c=c: e.transpose(
                        pT0[0:16, 128:256], SQ64[:, c * E16:(c + 1) * E16], IDN),
                      (sT, 1))
                    B.wait("vector", sT)
                    p("vector", lambda e: e.tensor_copy(ZST, pT0[0:16, 0:128]))
                    p("vector", lambda e: e.tensor_copy(ZSQT, pT0[0:16, 128:256]),
                      (sV, 1))
                    B.wait("tensor", sV)
                    p("tensor", lambda e: e.matmul(
                        pT1[:, 0:16], ZST, U16S, start=True, stop=True))
                    p("tensor", lambda e: e.matmul(
                        pT1[:, 16:32], ZSQT, U16S, start=True, stop=True), (sT, 1))
                    B.wait("vector", sT)
                    p("vector", lambda e, c=c: e.tensor_copy(
                        CS64[:, c * E16:(c + 1) * E16], pT1[:, 0:16]))
                    p("vector", lambda e, c=c: e.tensor_copy(
                        CSQ64[:, c * E16:(c + 1) * E16], pT1[:, 16:32]), (sV, 1))
                cs_done = sV.n

                B.wait("vector", sV, cs_done)
                vg(lambda e: e.tensor_tensor(MEAN64, CS64, KINV64, op=ALU.mult))
                vg(lambda e: e.tensor_tensor(WRK64, CSQ64, KINV64, op=ALU.mult))
                vg(lambda e: e.tensor_tensor(TAU64, MEAN64, MEAN64, op=ALU.mult))
                vg(lambda e: e.tensor_tensor(WRK64, WRK64, TAU64, op=ALU.subtract))
                vg(lambda e: e.tensor_tensor(WRK64, WRK64, KVAL64, op=ALU.mult))
                vg(lambda e: e.tensor_scalar(WRK64, WRK64, col(C_NONE), col(C_ONE),
                                             op0=ALU.mult, op1=ALU.add))
                vg(lambda e: e.tensor_tensor(WRK64, WRK64, KINV64, op=ALU.mult))
                vg(lambda e: e.tensor_scalar_max(WRK64, WRK64, col(C_ZERO)))
                B.wait("scalar", sV)
                p("scalar", lambda e: e.activation(WRK64, WRK64, AF.Sqrt), (sS, 1))
                B.wait("vector", sS)
                vg(lambda e: e.tensor_tensor(TAU64, MEAN64, WRK64, op=ALU.subtract))
                vg(lambda e: e.tensor_tensor(WRK64, TAU64, ZS, op=ALU.is_le))
                vg(lambda e: e.reduce_sum(supp4, seg(WRK64), axis=AX.X))
                vg(lambda e: e.tensor_scalar_add(supp4, supp4, col(C_NONE)))
                _ia, _ib = bc(seg(IOTA64), supp4)
                p("vector", lambda e, _ia=_ia, _ib=_ib: e.tensor_tensor(
                    seg(WRK64), _ia, _ib, op=ALU.is_equal), (sV, 1))
                B.wait("vector", sV)
                vg(lambda e: e.tensor_tensor(WRK64, WRK64, TAU64, op=ALU.mult))
                vg(lambda e: e.reduce_sum(supp4, seg(WRK64), axis=AX.X))
                _qa, _qb = bc(seg(PALL), supp4)
                p("vector", lambda e, _qa=_qa, _qb=_qb: e.tensor_tensor(
                    seg(AOUT), _qa, _qb, op=ALU.subtract), (sV, 1))
                B.wait("vector", sV)
                vg(lambda e: e.tensor_scalar_max(AOUT, AOUT, col(C_ZERO)))
                vg(lambda e: e.tensor_tensor(AOUT, AOUT, AOUT, op=ALU.mult))

                for c in range(GRP):
                    B.wait("tensor", sV)
                    p("tensor", lambda e, c=c: e.transpose(
                        pT0[0:16, 0:128], AOUT[:, c * E16:(c + 1) * E16], IDN),
                      (sT, 1))
                    B.wait("vector", sT)
                    p("vector", lambda e, c=c: e.tensor_copy(
                        ATG[:, c * 128:(c + 1) * 128], pT0[0:16, 0:128]), (sV, 1))
                    atg_done = sV.n
                    B.wait("tensor", sV, atg_done)
                    p("tensor", lambda e, c=c: e.matmul(
                        pB[:, 0:512], ATG[:, c * 128:(c + 1) * 128],
                        C12[:, 0:512], start=True, stop=True), (sT, 1))
                    B.wait("vector", sT)
                    p("vector", lambda e: e.tensor_copy(GB, pB[:, 0:512]), (sV, 1))
                    B.wait("vector", sV)
                    p("vector", lambda e, c=c: e.tensor_tensor(
                        Z1F[c], Z1F[c], GB, op=ALU.mult), (sV, 1))
                    z1_done = sV.n
                    B.wait("tensor", sV, z1_done)
                    for i in range(4):
                        p("tensor", lambda e, i=i, c=c: e.transpose(
                            pT0[:, i * 128:(i + 1) * 128],
                            Z1F[c][:, i * 128:(i + 1) * 128], IDN),
                          (sT, 1) if i == 3 else None)
                    B.wait("vector", sT)
                    p("vector", lambda e, c=c: e.tensor_copy(
                        Z1TH.rearrange("p (i t) -> p i t", i=4)[:, :,
                                                               c * 128:(c + 1) * 128],
                        pT0[:, 0:512].rearrange("p (i t) -> p i t", i=4)), (sV, 1))
                    B.wait("vector", sV)
                    p("vector", lambda e, c=c: e.tensor_tensor(
                        Z1TL.rearrange("p (i t) -> p i t", i=4)[:, :,
                                                               c * 128:(c + 1) * 128],
                        pT0[:, 0:512].rearrange("p (i t) -> p i t", i=4),
                        Z1TH.rearrange("p (i t) -> p i t", i=4)[:, :,
                                                                c * 128:(c + 1) * 128],
                        op=ALU.subtract), (sV, 1))

                gelu_dones = {}
                ht_dones = {}
                z2_dones = {}
                htsplit = {}
                NOC = O // 128

                dma_done = {}

                def emit_stream(oc):
                    if oc >= NOC:
                        return
                    buf = oc % 3
                    if oc >= 3:
                        B.wait("sync", sT, ht_dones[oc - 3])
                        B.wait("sync", sT, z2_dones[oc - 3])
                    else:
                        B.wait("sync", sT, prev_ht_tail[0])
                    p("sync", lambda e, oc=oc, buf=buf: e.dma_start(
                        out=WO1BH[buf].rearrange("p (i o) -> p i o", i=4),
                        in_=d_wo1h[:, oc * 128:(oc + 1) * 128].rearrange(
                            "(i p) o -> p i o", p=128)), (sWO, 16))
                    p("sync", lambda e, oc=oc, buf=buf: e.dma_start(
                        out=WO1BL[buf].rearrange("p (i o) -> p i o", i=4),
                        in_=d_wo1l[:, oc * 128:(oc + 1) * 128].rearrange(
                            "(i p) o -> p i o", p=128)), (sWO, 16))
                    p("sync", lambda e, oc=oc, buf=buf: e.dma_start(
                        out=W2SH[buf], in_=d_w2h[oc * 128:(oc + 1) * 128, :]),
                      (sWO, 16))
                    p("sync", lambda e, oc=oc, buf=buf: e.dma_start(
                        out=W2SL[buf], in_=d_w2l[oc * 128:(oc + 1) * 128, :]),
                      (sWO, 16))
                    dma_done[oc] = sWO.n

                def emit_ht(oc):
                    buf = oc % 3
                    B.wait("tensor", sV)
                    B.wait("tensor", sWO, dma_done[oc])
                    if oc >= 2:
                        B.wait("tensor", sS, gelu_dones[oc - 2])
                    bank = pC[:, (oc % 2) * 512:(oc % 2 + 1) * 512]
                    for i in range(4):
                        mm3(bank,
                            WO1BH[buf][:, i * 128:(i + 1) * 128],
                            WO1BL[buf][:, i * 128:(i + 1) * 128],
                            Z1TH[:, i * 512:(i + 1) * 512],
                            Z1TL[:, i * 512:(i + 1) * 512],
                            start=(i == 0), stop=(i == 3),
                            inc=(sT, 1) if i == 3 else None)
                    ht_dones[oc] = sT.n
                    hbuf = oc % 2
                    B.wait("scalar", sT, ht_dones[oc])
                    if oc >= 2:
                        B.wait("scalar", sT, z2_dones[oc - 2])
                    p("scalar", lambda e, hbuf=hbuf, bank=bank: e.activation(
                        HTB[hbuf], bank, AF.Gelu), (sS, 1))
                    gelu_dones[oc] = sS.n
                    B.wait("vector", sS, gelu_dones[oc])
                    if oc >= 2:
                        B.wait("vector", sT, z2_dones[oc - 2])
                    p("vector", lambda e, hbuf=hbuf: e.tensor_copy(HTH[hbuf], HTB[hbuf]),
                      (sV, 1))
                    B.wait("vector", sV)
                    p("vector", lambda e, hbuf=hbuf: e.tensor_tensor(
                        HTL[hbuf], HTB[hbuf], HTH[hbuf], op=ALU.subtract), (sV, 1))
                    htsplit[oc] = sV.n

                def emit_z2(oc):
                    buf = oc % 2
                    sbuf = oc % 3
                    B.wait("tensor", sV, htsplit[oc])
                    for c in range(GRP):
                        pz = (pA if c < 2 else pB)[:, (c % 2) * 512:(c % 2 + 1) * 512]
                        mm3(pz,
                            HTH[buf][:, c * 128:(c + 1) * 128],
                            HTL[buf][:, c * 128:(c + 1) * 128],
                            W2SH[sbuf], W2SL[sbuf],
                            start=(oc == 0), stop=False,
                            inc=(sT, 1) if c == GRP - 1 else None)
                    z2_dones[oc] = sT.n

                emit_stream(0)
                emit_stream(1)
                emit_stream(2)
                emit_ht(0)
                emit_ht(1)
                emit_z2(0)
                emit_stream(3)
                emit_ht(2)
                emit_z2(1)
                emit_stream(4)
                emit_lnblock(gg + 1)
                for oc in range(3, NOC):
                    emit_ht(oc)
                    emit_z2(oc - 1)
                    emit_stream(oc + 2)
                emit_z2(NOC - 1)
                prev_ht_tail[0] = ht_dones[NOC - 1]
                for c in range(GRP):
                    pz = (pA if c < 2 else pB)[:, (c % 2) * 512:(c % 2 + 1) * 512]
                    p("tensor", lambda e, pz=pz: e.matmul(
                        pz, ONESR, W2B, start=False, stop=True),
                      (sT, 1) if c == GRP - 1 else None)
                z2_done = sT.n

                g2_done = {}
                for c in range(GRP):
                    B.wait("tensor", sV)
                    p("tensor", lambda e, c=c: e.matmul(
                        pT1[:, 0:512], ATG[:, c * 128:(c + 1) * 128],
                        C12[:, 512:1024], start=True, stop=True), (sT, 1))
                    g2_done[c] = sT.n
                    pz = (pA if c < 2 else pB)[:, (c % 2) * 512:(c % 2 + 1) * 512]
                    B.wait("vector", sT, g2_done[c])
                    B.wait("vector", sT, z2_done)
                    p("vector", lambda e: e.tensor_copy(GB, pT1[:, 0:512]), (sV, 1))
                    B.wait("vector", sV)
                    p("vector", lambda e, pz=pz: e.tensor_tensor(
                        Z2S, pz, GB, op=ALU.mult), (sV, 1))
                    z2f_done = sV.n
                    B.wait("tensor", sV, z2f_done)
                    for i in range(4):
                        p("tensor", lambda e, i=i: e.transpose(
                            pT0[:, i * 128:(i + 1) * 128],
                            Z2S[:, i * 128:(i + 1) * 128], IDN),
                          (sT, 1) if i == 3 else None)
                    B.wait("vector", sT)
                    p("vector", lambda e: e.tensor_copy(Z2TH, pT0[:, 0:512]), (sV, 1))
                    B.wait("vector", sV)
                    p("vector", lambda e: e.tensor_tensor(
                        Z2TL, pT0[:, 0:512], Z2TH, op=ALU.subtract), (sV, 1))
                    z2t_done = sV.n
                    B.wait("tensor", sV, z2t_done)
                    B.wait("tensor", sS)
                    for h in range(2):
                        for i in range(4):
                            mm3(pC[:, h * 512:(h + 1) * 512],
                                Z2TH[:, i * 128:(i + 1) * 128],
                                Z2TL[:, i * 128:(i + 1) * 128],
                                WOUT2H[:, i * D + h * 512:i * D + (h + 1) * 512],
                                WOUT2L[:, i * D + h * 512:i * D + (h + 1) * 512],
                                start=(i == 0), stop=(i == 3),
                                inc=(sT, 1) if (i == 3 and h == 1) else None)
                    oo_done = sT.n
                    B.wait("vector", sT, oo_done)
                    B.wait("vector", sST)
                    p("vector", lambda e, c=c, xg=xg: e.tensor_tensor(
                        FOUT, pC, xg[c], op=ALU.add), (sV, 1))
                    fin_done[(gg, c)] = sV.n
                    JJ = gg * GRP + c
                    r0 = JJ * CH
                    B.wait("gpsimd", sV, fin_done[(gg, c)])
                    p("gpsimd", lambda e, r0=r0, dst=dst: e.dma_start(
                        out=dst[r0:r0 + CH, :], in_=FOUT), (sST, 16))

        def emit_pass_attn_b(src, dst):
            A = Region(arena, 0, ARENA_COLS, arena_bf)
            stage_f = [A.alloc(2048), A.alloc(2048)]
            stage_h = [A.alloc_bf(2048), A.alloc_bf(2048)]
            stage_l = [A.alloc_bf(2048), A.alloc_bf(2048)]
            B.wait("sync", sV)
            B.wait("sync", sT)
            B.wait("sync", sS)
            B.wait("vector", sST)
            B.wait("scalar", sST)
            split_sh = emit_split(
                [(d_wq, D, D, d_qh, d_ql), (d_wk, D, D, d_kh, d_kl),
                 (d_wv, D, D, d_vh, d_vl), (d_wo, D, D, d_oh, d_ol),
                 (d_wg1, D, LR, d_g1hh, d_g1ll), (d_wg2, LR, D, d_g2hh, d_g2ll)],
                stage_f, stage_h, stage_l)
            A.off = A.base

            WQH = A.alloc_bf(DC * D)
            WQL = A.alloc_bf(DC * D)
            WKH = A.alloc_bf(DC * D)
            WKL = A.alloc_bf(DC * D)
            WVH = A.alloc_bf(DC * D)
            WVL = A.alloc_bf(DC * D)
            WOH = A.alloc_bf(DC * D)
            WOL = A.alloc_bf(DC * D)
            WG1H = A.alloc_bf(DC * LR)
            WG1L = A.alloc_bf(DC * LR)
            WG2H = A.alloc_bf(D)
            WG2L = A.alloc_bf(D)
            UTH = A.alloc_bf(D)
            UTL = A.alloc_bf(D)
            G1TH = A.alloc_bf(LR)
            G1TL = A.alloc_bf(LR)
            TRILS = A.alloc(128)
            TRILB = A.alloc_bf(D)
            XB2 = A.alloc(D)
            UB2 = A.alloc(D)
            QB = A.alloc(D)
            eb_off = A.off
            EB = A.alloc(D)
            VB2 = A.alloc(D)
            PB2 = A.alloc(D)
            PT2 = A.alloc(D)
            ET2 = A.alloc(D)
            SCM = A.alloc(D)
            STB = A.alloc(D)
            GSB = A.alloc(D)
            OB = A.alloc(D)
            AB2 = OB
            CARRY = A.alloc(D, parts=1)
            STT = A.alloc(8)
            s1 = STT[:, 0:1]
            mean = STT[:, 1:2]
            var = STT[:, 2:3]
            rstd = STT[:, 3:4]
            sd = STT[:, 4:5]
            ssq = STT[:, 5:6]
            LNSQ2 = QB
            FOUT2 = VB2
            G1B = SCM[:, 0:LR]
            OTH = arena_bf[0:128, 2 * eb_off:2 * eb_off + D]
            OTL = arena_bf[0:128, 2 * eb_off + D:2 * eb_off + 2 * D]

            B.wait("sync", sSH, split_sh)
            for dstw, srcw in ((WQH, d_qh), (WQL, d_ql), (WKH, d_kh), (WKL, d_kl),
                               (WVH, d_vh), (WVL, d_vl), (WOH, d_oh), (WOL, d_ol),
                               (WG1H, d_g1hh), (WG1L, d_g1ll)):
                nrows = srcw.shape[0] // 128
                p("sync", lambda e, dstw=dstw, srcw=srcw, nrows=nrows: e.dma_start(
                    out=dstw.rearrange("p (i r) -> p i r", i=nrows),
                    in_=srcw.rearrange("(i p) r -> p i r", p=128)), (sLW, 16))
            p("sync", lambda e: e.dma_start(out=WG2H, in_=d_g2hh), (sLW, 16))
            p("sync", lambda e: e.dma_start(out=WG2L, in_=d_g2ll), (sLW, 16))
            p("sync", lambda e: e.dma_start(out=TRILS, in_=d_tril), (sLW, 16))
            p("sync", lambda e: e.dma_start(out=TRILB, in_=d_trilb), (sLW, 16))
            w_ready = sLW.n
            entry_sST = sST.n
            B.wait("vector", sST, entry_sST)
            B.wait("scalar", sST, entry_sST)

            NTOT = NSEQ * NCH
            ln_done = {}
            trans_done = {}
            fin_done = {}
            p_done = {}
            oo_done = {}
            ot_evac = {-1: 0}

            def emit_load(JJ):
                if JJ >= NTOT:
                    return
                r0 = JJ * CH
                B.wait("sync", sV, fin_done.get(JJ - 1, 0))
                B.wait("sync", sST, entry_sST)
                p("sync", lambda e, r0=r0: e.dma_start(
                    out=XB2, in_=src[r0:r0 + CH, :]), (sLX, 16))

            def emit_ln(JJ):
                if JJ >= NTOT:
                    return
                B.wait("vector", sLX)
                p("vector", lambda e: e.reduce_sum(s1, XB2, axis=AX.X), (sV, 1))
                B.wait("scalar", sLX)
                B.wait("scalar", sV, p_done.get(JJ - 1, 0))
                p("scalar", lambda e: e.activation(
                    LNSQ2, XB2, AF.Square, accum_out=ssq), (sS, 1))
                B.wait("vector", sV)
                B.wait("vector", sS)
                p("vector", lambda e: e.tensor_scalar_mul(mean, s1, col(C_NI1024)))
                p("vector", lambda e: e.tensor_scalar_mul(var, ssq, col(C_I1024)),
                  (sV, 1))
                B.wait("vector", sV)
                p("vector", lambda e: e.tensor_tensor(sd, mean, mean, op=ALU.mult),
                  (sV, 1))
                B.wait("vector", sV)
                p("vector", lambda e: e.tensor_tensor(var, var, sd, op=ALU.subtract),
                  (sV, 1))
                B.wait("scalar", sV)
                p("scalar", lambda e: e.activation(sd, var, AF.Sqrt, bias=col(C_EPS)),
                  (sS, 1))
                B.wait("vector", sS)
                p("vector", lambda e: e.reciprocal(rstd, sd), (sV, 1))
                B.wait("vector", sV)
                B.wait("vector", sT, trans_done.get(JJ - 1, 0))
                p("vector", lambda e: e.tensor_scalar(
                    UB2, XB2, mean, rstd, op0=ALU.add, op1=ALU.mult), (sV, 1))
                ln_done[JJ] = sV.n

            emit_load(0)
            emit_ln(0)

            for JJ in range(NTOT):
                Jloc = JJ % NCH
                r0 = JJ * CH
                if Jloc == 0:
                    B.wait("vector", sSH)
                    p("vector", lambda e: e.memset(CARRY, 0.0), (sV, 1))

                B.wait("tensor", sV, ln_done[JJ])
                B.wait("tensor", sV, ot_evac.get(JJ - 1, 0))
                B.wait("tensor", sS)
                for h in range(2):
                    pdst = (pT0 if h == 0 else pT1)
                    for i in range(4):
                        ii = h * 4 + i
                        p("tensor", lambda e, ii=ii, pdst=pdst, i=i: e.transpose(
                            pdst[:, i * 128:(i + 1) * 128],
                            UB2[:, ii * 128:(ii + 1) * 128], IDN),
                          (sT, 1) if i == 3 else None)
                    B.wait("vector", sT)
                    p("vector", lambda e, h=h, pdst=pdst: e.tensor_copy(
                        UTH[:, h * 512:(h + 1) * 512], pdst), (sV, 1))
                    B.wait("vector", sV)
                    p("vector", lambda e, h=h, pdst=pdst: e.tensor_tensor(
                        UTL[:, h * 512:(h + 1) * 512], pdst,
                        UTH[:, h * 512:(h + 1) * 512], op=ALU.subtract), (sV, 1))
                trans_done[JJ] = sT.n
                ut_done = sV.n

                B.wait("tensor", sLW, w_ready)
                B.wait("tensor", sV, ut_done)
                for wh, wl, pd in ((WQH, WQL, pA), (WKH, WKL, pB)):
                    for h in range(2):
                        for i in range(DC):
                            mm3(pd[:, h * 512:(h + 1) * 512],
                                UTH[:, i * 128:(i + 1) * 128],
                                UTL[:, i * 128:(i + 1) * 128],
                                wh[:, i * D + h * 512:i * D + (h + 1) * 512],
                                wl[:, i * D + h * 512:i * D + (h + 1) * 512],
                                start=(i == 0), stop=(i == DC - 1),
                                inc=(sT, 1) if (i == DC - 1 and h == 1) else None)
                q_done, k_done = sT.n - 1, sT.n
                B.wait("scalar", sT, q_done)
                B.wait("scalar", sV, p_done.get(JJ - 1, 0))
                p("scalar", lambda e: e.activation(QB, pA, AF.Silu), (sS, 1))
                q_evac = sS.n
                B.wait("scalar", sT, k_done)
                B.wait("scalar", sT, oo_done.get(JJ - 1, 0))
                p("scalar", lambda e: e.activation(EB, pB, AF.Exp), (sS, 1))
                k_evac = sS.n

                B.wait("tensor", sS, q_evac)
                for h in range(2):
                    for i in range(DC):
                        mm3(pA[:, h * 512:(h + 1) * 512],
                            UTH[:, i * 128:(i + 1) * 128],
                            UTL[:, i * 128:(i + 1) * 128],
                            WVH[:, i * D + h * 512:i * D + (h + 1) * 512],
                            WVL[:, i * D + h * 512:i * D + (h + 1) * 512],
                            start=(i == 0), stop=(i == DC - 1),
                            inc=(sT, 1) if (i == DC - 1 and h == 1) else None)
                v_done = sT.n
                B.wait("vector", sT, v_done)
                B.wait("vector", sST)
                p("vector", lambda e: e.tensor_copy(VB2, pA), (sV, 1))
                v_evac = sV.n

                for i in range(DC):
                    mm3(pT1[:, 0:128],
                        UTH[:, i * 128:(i + 1) * 128],
                        UTL[:, i * 128:(i + 1) * 128],
                        WG1H[:, i * LR:(i + 1) * LR],
                        WG1L[:, i * LR:(i + 1) * LR],
                        start=(i == 0), stop=(i == DC - 1),
                        inc=(sT, 1) if i == DC - 1 else None)
                B.wait("vector", sT)
                p("vector", lambda e: e.tensor_copy(G1B, pT1[:, 0:128]), (sV, 1))
                B.wait("tensor", sV)
                p("tensor", lambda e: e.transpose(pT0[:, 0:128], G1B, IDN), (sT, 1))
                B.wait("vector", sT)
                p("vector", lambda e: e.tensor_copy(G1TH, pT0[:, 0:128]), (sV, 1))
                B.wait("vector", sV)
                p("vector", lambda e: e.tensor_tensor(
                    G1TL, pT0[:, 0:128], G1TH, op=ALU.subtract), (sV, 1))
                B.wait("tensor", sV)
                B.wait("tensor", sS, k_evac)
                for h in range(2):
                    mm3(pB[:, h * 512:(h + 1) * 512], G1TH, G1TL,
                        WG2H[:, h * 512:(h + 1) * 512],
                        WG2L[:, h * 512:(h + 1) * 512],
                        start=True, stop=True,
                        inc=(sT, 1) if h == 1 else None)
                gate_done = sT.n
                B.wait("scalar", sT, gate_done)
                p("scalar", lambda e: e.activation(GSB, pB, AF.Silu), (sS, 1))
                gate_evac = sS.n

                B.wait("tensor", sS, gate_evac)
                B.wait("tensor", sSH)
                B.wait("tensor", sV)
                for h in range(2):
                    p("tensor", lambda e, h=h: e.matmul(
                        pB[:, h * 512:(h + 1) * 512], TRILS,
                        EB[:, h * 512:(h + 1) * 512], start=True, stop=False))
                    p("tensor", lambda e, h=h: e.matmul(
                        pB[:, h * 512:(h + 1) * 512], ONESR,
                        CARRY[0:1, h * 512:(h + 1) * 512],
                        start=False, stop=True),
                      (sT, 1) if h == 1 else None)
                a_done = sT.n
                B.wait("vector", sT, a_done)
                p("vector", lambda e: e.tensor_copy(AB2, pB), (sV, 1))
                a_evac = sV.n
                B.wait("sync", sV, a_evac)
                p("sync", lambda e: e.dma_start(out=CARRY, in_=AB2[127:128, :]),
                  (sSH, 16))
                p("vector", lambda e: e.reciprocal(PB2, AB2), (sV, 1))
                B.wait("vector", sS, q_evac)
                p("vector", lambda e: e.scalar_tensor_tensor(
                    PB2, PB2, col(C_BSCALE), QB, op0=ALU.mult, op1=ALU.mult),
                  (sV, 1))
                p_done[JJ] = sV.n

                B.wait("tensor", sV, p_done[JJ])
                for src_b, dst_b in ((PB2, PT2), (EB, ET2)):
                    B.wait("tensor", sV)
                    for h in range(2):
                        pdst = (pT0 if h == 0 else pT1)
                        for i in range(4):
                            ii = h * 4 + i
                            p("tensor",
                              lambda e, ii=ii, pdst=pdst, i=i, src_b=src_b:
                              e.transpose(pdst[:, i * 128:(i + 1) * 128],
                                          src_b[:, ii * 128:(ii + 1) * 128], IDN),
                              (sT, 1) if i == 3 else None)
                        B.wait("vector", sT)
                        p("vector", lambda e, h=h, pdst=pdst, dst_b=dst_b:
                          e.tensor_copy(dst_b[:, h * 512:(h + 1) * 512], pdst),
                          (sV, 1))
                pt_done = sV.n

                B.wait("tensor", sV, pt_done)
                for h in range(8):
                    p("tensor", lambda e, h=h: e.matmul(
                        pB[:, h * 128:(h + 1) * 128],
                        ET2[:, h * 128:(h + 1) * 128],
                        PT2[:, h * 128:(h + 1) * 128],
                        start=True, stop=True),
                      (sT, 1) if h == 7 else None)
                sc_done = sT.n
                B.wait("vector", sT, sc_done)
                p("vector", lambda e: e.tensor_tensor(
                    SCM, pB, TRILB, op=ALU.mult), (sV, 1))
                if Jloc > 0:
                    p("vector", lambda e: e.tensor_copy(STB, pC), (sV, 1))
                st_copy = sV.n

                B.wait("tensor", sV, st_copy)
                for h in range(8):
                    p("tensor", lambda e, h=h, Jloc=Jloc: e.matmul(
                        pA[:, h * 128:(h + 1) * 128],
                        SCM[:, h * 128:(h + 1) * 128],
                        VB2[:, h * 128:(h + 1) * 128],
                        start=True, stop=(Jloc == 0)),
                      (sT, 1) if (h == 7 and Jloc == 0) else None)
                    if Jloc > 0:
                        p("tensor", lambda e, h=h: e.matmul(
                            pA[:, h * 128:(h + 1) * 128],
                            PT2[:, h * 128:(h + 1) * 128],
                            STB[:, h * 128:(h + 1) * 128],
                            start=False, stop=True),
                          (sT, 1) if h == 7 else None)
                o_done = sT.n

                for h in range(8):
                    p("tensor", lambda e, h=h, Jloc=Jloc: e.matmul(
                        pC[:, h * 128:(h + 1) * 128],
                        EB[:, h * 128:(h + 1) * 128],
                        VB2[:, h * 128:(h + 1) * 128],
                        start=(Jloc == 0 and h % 4 == 0),
                        stop=(Jloc == NCH - 1 and h % 4 == 3)),
                      (sT, 1) if h == 7 else None)
                ds_done = sT.n

                B.wait("scalar", sT, o_done)
                B.wait("scalar", sV, p_done[JJ])
                p("scalar", lambda e: e.activation(
                    LNSQ2, pA, AF.Square, accum_out=ssq), (sS, 1))
                B.wait("scalar", sS)
                p("scalar", lambda e: e.activation(
                    sd, ssq, AF.Sqrt, bias=col(C_EPS), scale=col(C_I1024)),
                  (sS, 1))
                B.wait("vector", sS)
                p("vector", lambda e: e.reciprocal(rstd, sd), (sV, 1))
                B.wait("vector", sV)
                B.wait("vector", sS, gate_evac)
                p("vector", lambda e: e.scalar_tensor_tensor(
                    OB, pA, rstd, GSB, op0=ALU.mult, op1=ALU.mult), (sV, 1))
                ob_done = sV.n

                B.wait("tensor", sV, ob_done)
                B.wait("tensor", sT, ds_done)
                for h in range(2):
                    pdst = (pT0 if h == 0 else pT1)
                    for i in range(4):
                        ii = h * 4 + i
                        p("tensor", lambda e, ii=ii, pdst=pdst, i=i: e.transpose(
                            pdst[:, i * 128:(i + 1) * 128],
                            OB[:, ii * 128:(ii + 1) * 128], IDN),
                          (sT, 1) if i == 3 else None)
                    B.wait("vector", sT)
                    p("vector", lambda e, h=h, pdst=pdst: e.tensor_copy(
                        OTH[:, h * 512:(h + 1) * 512], pdst), (sV, 1))
                    B.wait("vector", sV)
                    p("vector", lambda e, h=h, pdst=pdst: e.tensor_tensor(
                        OTL[:, h * 512:(h + 1) * 512], pdst,
                        OTH[:, h * 512:(h + 1) * 512], op=ALU.subtract), (sV, 1))
                ot_evac[JJ] = sV.n

                B.wait("tensor", sV, ot_evac[JJ])
                for h in range(2):
                    for i in range(DC):
                        mm3(pB[:, h * 512:(h + 1) * 512],
                            OTH[:, i * 128:(i + 1) * 128],
                            OTL[:, i * 128:(i + 1) * 128],
                            WOH[:, i * D + h * 512:i * D + (h + 1) * 512],
                            WOL[:, i * D + h * 512:i * D + (h + 1) * 512],
                            start=(i == 0), stop=(i == DC - 1),
                            inc=(sT, 1) if (i == DC - 1 and h == 1) else None)
                oo_done[JJ] = sT.n
                B.wait("vector", sT, oo_done[JJ])
                B.wait("vector", sST)
                p("vector", lambda e: e.tensor_tensor(
                    FOUT2, pB, XB2, op=ALU.add), (sV, 1))
                fin_done[JJ] = sV.n
                B.wait("gpsimd", sV, fin_done[JJ])
                p("gpsimd", lambda e, r0=r0, dst=dst: e.dma_start(
                    out=dst[r0:r0 + CH, :], in_=FOUT2), (sST, 16))

                emit_load(JJ + 1)
                emit_ln(JJ + 1)

        chain = []
        if n_passes >= 1:
            chain.append(emit_pass_attn_a)
        if n_passes >= 2:
            chain.append(emit_pass_ffn)
        if n_passes >= 3:
            chain.append(emit_pass_attn_b)
        if n_passes >= 4:
            chain.append(emit_pass_ffn)
        srcs = [x_in] + [resid] * (len(chain) - 1)
        for i, fn in enumerate(chain):
            dst = out if i == len(chain) - 1 else resid
            fn(srcs[i], dst)

        B.wait("gpsimd", sST)

        import contextlib
        with contextlib.ExitStack() as stack:
            for s in B.sems:
                s.handle = stack.enter_context(nc.semaphore(s.name))
            for eng in ENGINES:
                stream = B.streams[eng]
                getattr(block, eng)(lambda e, stream=stream: [f(e) for f in stream])

    return nc



def _consts():
    idn = np.eye(128, dtype=np.float32)
    ccols = np.zeros((128, 16), dtype=np.float32)
    vals = {C_EPS: 1e-5, C_I1024: 1.0 / 1024.0, C_NI1024: -1.0 / 1024.0,
            C_I16: 1.0 / 16.0, C_HALF: 0.5, C_ONE: 1.0, C_NONE: -1.0,
            C_BSCALE: 1.0 / np.sqrt(128.0), C_ZERO: 0.0, C_NI16: -1.0 / 16.0,
            C_LNBS: float(-0.5 * np.log(128.0))}
    for k, v in vals.items():
        ccols[:, k] = v
    ones = np.ones((1, 128), dtype=np.float32)
    k_i = np.arange(128)[:, None]
    c_i = np.arange(128)[None, :]
    diffa = np.where(c_i > k_i, (c_i - 1 - k_i).astype(np.float64), 1e9)
    diffa = diffa.astype(np.float32)
    c_col = np.arange(128)[:, None]
    j_row = np.arange(NCH)[None, :]
    cmw = (127 - c_col - 128 * j_row).astype(np.float32)
    cmi = (c_col + 128 * (j_row - 1)).astype(np.float32)
    cmi[:, 0] = 1e9
    tril = (k_i <= c_i).astype(np.float32)
    e = np.arange(E16)
    u16 = (e[:, None] <= e[None, :]).astype(np.float32)
    e64c = np.zeros((128, 192), dtype=np.float32)
    e64c[:, 0:64] = np.tile(e, 4)[None, :]
    e64c[:, 64:128] = np.tile(1.0 / (e + 1.0), 4)[None, :]
    e64c[:, 128:192] = np.tile(e + 1.0, 4)[None, :]
    import ml_dtypes
    trilb = np.tile(tril, (1, 8)).astype(ml_dtypes.bfloat16)
    return dict(idn=idn, ccols=ccols, onesr=ones, diffa=diffa, cmw=cmw,
                cmi=cmi, tril=tril, u16=u16, e64c=e64c, trilb=trilb)


def prep_in_maps(inputs, n_passes=4):
    f32 = lambda a: np.ascontiguousarray(np.asarray(a, dtype=np.float32))
    x = f32(inputs["L0_attn_a_out"])
    wwr = f32(np.asarray(inputs["L0_attn_a_W_write"]).T)
    wrd = f32(np.asarray(inputs["L0_attn_a_W_read"]).T)
    scal = np.stack([np.float32(inputs["L0_attn_a_decay"]),
                     np.float32(inputs["L0_attn_a_log_alpha"])]).reshape(1, 2)
    scal = f32(scal)
    wq = f32(np.asarray(inputs["L0_attn_b_Wq"]).T)
    wk = f32(np.asarray(inputs["L0_attn_b_Wk"]).T)
    wv = f32(np.asarray(inputs["L0_attn_b_Wv"]).T)
    wo = f32(np.asarray(inputs["L0_attn_b_Wo"]).T)
    wg1 = f32(np.asarray(inputs["L0_attn_b_Wg1"]).T)
    wg2 = f32(np.asarray(inputs["L0_attn_b_Wg2"]).T)
    proj = f32(inputs["L0_ffn_proj"])
    c1c2 = f32(np.concatenate([np.asarray(inputs["L0_ffn_C1"]),
                               np.asarray(inputs["L0_ffn_C2"])], axis=1))
    win1_full = np.asarray(inputs["L0_ffn_Win1"])
    win1 = f32(win1_full[:D])
    w1b = f32(win1_full[D:D + 1])
    wout1 = f32(np.asarray(inputs["L0_ffn_Wout1"]).T)
    win2_full = np.asarray(inputs["L0_ffn_Win2"])
    win2 = f32(win2_full[:O])
    w2b = f32(win2_full[O:O + 1])
    wout2 = f32(np.asarray(inputs["L0_ffn_Wout2"]).T)
    cst = _consts()

    shared = dict(wwr=wwr, wrd=wrd, scal=scal, wq=wq, wk=wk, wv=wv, wo=wo,
                  wg1=wg1, wg2=wg2, proj=proj, c1c2=c1c2, win1=win1, w1b=w1b,
                  wout1=wout1, win2=win2, w2b=w2b, wout2=wout2, **cst)
    in_maps = []
    for c in range(NCORES):
        m = dict(shared)
        m["x_in"] = np.ascontiguousarray(
            x[c * NSEQ:(c + 1) * NSEQ].reshape(NSEQ * T, D))
        in_maps.append(m)
    return in_maps


def kernel(**inputs):
    in_maps = prep_in_maps(inputs)
    nc = build_program(n_passes=4)
    res = run_bass_kernel_spmd(nc, in_maps, core_ids=list(range(NCORES)))
    parts = [res.results[c]["out"].reshape(NSEQ, T, D) for c in range(NCORES)]
    return np.concatenate(parts, axis=0)


# version 13
# speedup vs baseline: 1.0547x; 1.0547x over previous
import numpy as np
import concourse.bass as bass
import concourse.mybir as mybir
from concourse.bass_utils import run_bass_kernel_spmd

F32 = mybir.dt.float32
BF16 = mybir.dt.bfloat16
AF = mybir.ActivationFunctionType
ALU = mybir.AluOpType
AX = mybir.AxisListType

NCORES = 8
NSEQ = 2
T = 2048
D = 1024
CH = 128
NCH = T // CH
DC = D // 128
R = 512
O = 4096
E16 = 16
LR = 128

C_EPS = 0
C_I1024 = 1
C_NI1024 = 2
C_I16 = 3
C_HALF = 4
C_ONE = 5
C_NONE = 6
C_BSCALE = 7
C_ZERO = 8
C_NI16 = 9
C_LNBS = 10

ENGINES = ("sync", "tensor", "vector", "scalar", "gpsimd")


class Sem:
    __slots__ = ("name", "n", "handle")

    def __init__(self, name):
        self.name = name
        self.n = 0
        self.handle = None


class KB:

    def __init__(self):
        self.streams = {e: [] for e in ENGINES}
        self.sems = []

    def sem(self, name):
        s = Sem(name)
        self.sems.append(s)
        return s

    def em(self, eng, fn, inc=None):
        if inc is None:
            self.streams[eng].append(fn)
        else:
            sem, k = inc
            sem.n += k
            self.streams[eng].append(
                lambda e, fn=fn, sem=sem, k=k: fn(e).then_inc(sem.handle, k))

    def wait(self, eng, sem, val=None):
        v = sem.n if val is None else val
        assert v <= sem.n, f"wait on future count: {sem.name} {v} > {sem.n}"
        self.streams[eng].append(lambda e, sem=sem, v=v: e.wait_ge(sem.handle, v))


class Region:

    def __init__(self, ap_full, base, size, ap_bf=None):
        self.ap = ap_full
        self.ap_bf = ap_bf
        self.base = base
        self.off = base
        self.size = size

    def alloc(self, ncols, parts=128):
        a = self.ap
        o = self.off
        self.off += ncols
        assert self.off - self.base <= self.size, "arena region overflow"
        return a[0:parts, o:o + ncols]

    def alloc_bf(self, ncols_bf, parts=128):
        ncf = (ncols_bf + 1) // 2
        o = self.off
        self.off += ncf
        assert self.off - self.base <= self.size, "arena region overflow"
        return self.ap_bf[0:parts, 2 * o:2 * o + ncols_bf]


def build_program(n_passes=4):
    nc = bass.Bass("TRN2", target_bir_lowering=False)

    x_in = nc.dram_tensor("x_in", [NSEQ * T, D], F32, kind="ExternalInput").ap()
    resid = nc.dram_tensor("resid", [NSEQ * T, D], F32, kind="Internal").ap()
    out = nc.dram_tensor("out", [NSEQ * T, D], F32, kind="ExternalOutput").ap()

    d_wwr = nc.dram_tensor("wwr", [D, D], F32, kind="ExternalInput").ap()
    d_wrd = nc.dram_tensor("wrd", [D, D], F32, kind="ExternalInput").ap()
    d_scal = nc.dram_tensor("scal", [1, 2], F32, kind="ExternalInput").ap()

    d_wq = nc.dram_tensor("wq", [D, D], F32, kind="ExternalInput").ap()
    d_wk = nc.dram_tensor("wk", [D, D], F32, kind="ExternalInput").ap()
    d_wv = nc.dram_tensor("wv", [D, D], F32, kind="ExternalInput").ap()
    d_wo = nc.dram_tensor("wo", [D, D], F32, kind="ExternalInput").ap()
    d_wg1 = nc.dram_tensor("wg1", [D, LR], F32, kind="ExternalInput").ap()
    d_wg2 = nc.dram_tensor("wg2", [LR, D], F32, kind="ExternalInput").ap()

    d_proj = nc.dram_tensor("proj", [D, E16], F32, kind="ExternalInput").ap()
    d_c1c2 = nc.dram_tensor("c1c2", [E16, 2 * R], F32, kind="ExternalInput").ap()
    d_win1 = nc.dram_tensor("win1", [D, R], F32, kind="ExternalInput").ap()
    d_w1b = nc.dram_tensor("w1b", [1, R], F32, kind="ExternalInput").ap()
    d_wout1 = nc.dram_tensor("wout1", [R, O], F32, kind="ExternalInput").ap()
    d_win2 = nc.dram_tensor("win2", [O, R], F32, kind="ExternalInput").ap()
    d_w2b = nc.dram_tensor("w2b", [1, R], F32, kind="ExternalInput").ap()
    d_wout2 = nc.dram_tensor("wout2", [R, D], F32, kind="ExternalInput").ap()

    d_idn = nc.dram_tensor("idn", [128, 128], F32, kind="ExternalInput").ap()
    d_ccols = nc.dram_tensor("ccols", [128, 16], F32, kind="ExternalInput").ap()
    d_ones = nc.dram_tensor("onesr", [1, 128], F32, kind="ExternalInput").ap()
    d_diffa = nc.dram_tensor("diffa", [128, 128], F32, kind="ExternalInput").ap()
    d_cmw = nc.dram_tensor("cmw", [128, NCH], F32, kind="ExternalInput").ap()
    d_cmi = nc.dram_tensor("cmi", [128, NCH], F32, kind="ExternalInput").ap()
    d_tril = nc.dram_tensor("tril", [128, 128], F32, kind="ExternalInput").ap()
    d_trilb = nc.dram_tensor("trilb", [128, D], BF16, kind="ExternalInput").ap()
    d_u16 = nc.dram_tensor("u16", [E16, E16], F32, kind="ExternalInput").ap()
    d_e64c = nc.dram_tensor("e64c", [128, 192], F32, kind="ExternalInput").ap()
    d_w1h = nc.dram_tensor("w1h", [D, R], BF16, kind="Internal").ap()
    d_w1l = nc.dram_tensor("w1l", [D, R], BF16, kind="Internal").ap()
    d_w2h = nc.dram_tensor("w2h", [O, R], BF16, kind="Internal").ap()
    d_w2l = nc.dram_tensor("w2l", [O, R], BF16, kind="Internal").ap()
    d_wo1h = nc.dram_tensor("wo1h", [R, O], BF16, kind="Internal").ap()
    d_wo1l = nc.dram_tensor("wo1l", [R, O], BF16, kind="Internal").ap()
    d_wt2h = nc.dram_tensor("wt2h", [R, D], BF16, kind="Internal").ap()
    d_wt2l = nc.dram_tensor("wt2l", [R, D], BF16, kind="Internal").ap()
    d_pjh = nc.dram_tensor("pjh", [D, E16], BF16, kind="Internal").ap()
    d_pjl = nc.dram_tensor("pjl", [D, E16], BF16, kind="Internal").ap()
    d_qh = nc.dram_tensor("qh", [D, D], BF16, kind="Internal").ap()
    d_ql = nc.dram_tensor("ql", [D, D], BF16, kind="Internal").ap()
    d_kh = nc.dram_tensor("kh", [D, D], BF16, kind="Internal").ap()
    d_kl = nc.dram_tensor("kl", [D, D], BF16, kind="Internal").ap()
    d_vh = nc.dram_tensor("vh", [D, D], BF16, kind="Internal").ap()
    d_vl = nc.dram_tensor("vl", [D, D], BF16, kind="Internal").ap()
    d_oh = nc.dram_tensor("oh", [D, D], BF16, kind="Internal").ap()
    d_ol = nc.dram_tensor("ol", [D, D], BF16, kind="Internal").ap()
    d_g1hh = nc.dram_tensor("g1hh", [D, LR], BF16, kind="Internal").ap()
    d_g1ll = nc.dram_tensor("g1ll", [D, LR], BF16, kind="Internal").ap()
    d_g2hh = nc.dram_tensor("g2hh", [LR, D], BF16, kind="Internal").ap()
    d_g2ll = nc.dram_tensor("g2ll", [LR, D], BF16, kind="Internal").ap()
    d_wwrh = nc.dram_tensor("wwrh", [D, D], BF16, kind="Internal").ap()
    d_wwrl = nc.dram_tensor("wwrl", [D, D], BF16, kind="Internal").ap()
    d_wrdh = nc.dram_tensor("wrdh", [D, D], BF16, kind="Internal").ap()
    d_wrdl = nc.dram_tensor("wrdl", [D, D], BF16, kind="Internal").ap()

    B = KB()

    sLX = B.sem("sLX")
    sLW = B.sem("sLW")
    sSH = B.sem("sSH")
    sST = B.sem("sST")
    sT = B.sem("sT")
    sV = B.sem("sV")
    sS = B.sem("sS")
    sWO = B.sem("sWO")

    GLOBAL_COLS = 512
    ARENA_COLS = 52168

    with (
        nc.sbuf_tensor("glob", [128, GLOBAL_COLS], F32) as glob_t,
        nc.sbuf_tensor("arena", [128, ARENA_COLS], F32) as arena_t,
        nc.psum_tensor("ps0", [128, 512], F32) as ps0_t,
        nc.psum_tensor("ps1", [128, 512], F32) as ps1_t,
        nc.psum_tensor("pA", [128, 1024], F32) as pA_t,
        nc.psum_tensor("pB", [128, 1024], F32) as pB_t,
        nc.psum_tensor("pC", [128, 1024], F32) as pC_t,
        nc.Block() as block,
    ):
        glob = glob_t[:, :]
        arena = arena_t[:, :]
        arena_bf = arena_t.ap().bitcast(BF16)
        pT0 = ps0_t[:, :]
        pT1 = ps1_t[:, :]
        pA = pA_t[:, :]
        pB = pB_t[:, :]
        pC = pC_t[:, :]

        G = Region(glob, 0, GLOBAL_COLS)
        IDN = G.alloc(128)
        CC = G.alloc(16)
        ONESR = G.alloc(128, parts=1)
        SCAL = G.alloc(2, parts=1)
        LGC = G.alloc(1)
        ALC = G.alloc(1)
        SCW = G.alloc(NCH)
        SCI = G.alloc(NCH)
        MPA = G.alloc(128)

        def col(i):
            return CC[:, i:i + 1]

        def p(eng, fn, inc=None):
            B.em(eng, fn, inc)

        p("sync", lambda e: e.dma_start(out=IDN, in_=d_idn), (sLW, 16))
        p("sync", lambda e: e.dma_start(out=CC, in_=d_ccols), (sLW, 16))
        p("sync", lambda e: e.dma_start(out=ONESR, in_=d_ones), (sLW, 16))
        p("sync", lambda e: e.dma_start(out=SCAL, in_=d_scal), (sLW, 16))

        PR = Region(arena, 0, 512)
        DIFFA = PR.alloc(128)
        CMW = PR.alloc(NCH)
        CMI = PR.alloc(NCH)
        p("sync", lambda e: e.dma_start(out=DIFFA, in_=d_diffa), (sLW, 16))
        p("sync", lambda e: e.dma_start(out=CMW, in_=d_cmw), (sLW, 16))
        p("sync", lambda e: e.dma_start(out=CMI, in_=d_cmi), (sLW, 16))

        B.wait("scalar", sLW)
        sig1 = SCAL[0:1, 0:1]
        lg1 = ONESR
        TMP2 = G.alloc(4, parts=1)
        p("scalar", lambda e: e.activation(TMP2[0:1, 0:1], SCAL[0:1, 0:1], AF.Sigmoid),
          (sS, 1))
        B.wait("scalar", sS)
        p("scalar", lambda e: e.activation(TMP2[0:1, 1:2], TMP2[0:1, 0:1], AF.Ln),
          (sS, 1))
        B.wait("scalar", sS)
        p("scalar", lambda e: e.activation(TMP2[0:1, 2:3], SCAL[0:1, 1:2], AF.Exp),
          (sS, 1))
        B.wait("tensor", sS)
        B.wait("tensor", sLW)
        p("tensor", lambda e: e.matmul(pT0[:, 0:1], ONESR, TMP2[0:1, 1:2],
                                       start=True, stop=True))
        p("tensor", lambda e: e.matmul(pT0[:, 1:2], ONESR, TMP2[0:1, 2:3],
                                       start=True, stop=True), (sT, 1))
        B.wait("vector", sT)
        p("vector", lambda e: e.tensor_copy(LGC, pT0[:, 0:1]))
        p("vector", lambda e: e.tensor_copy(ALC, pT0[:, 1:2]), (sV, 1))
        B.wait("vector", sV)
        p("vector", lambda e: e.tensor_scalar_mul(SCW, CMW, LGC))
        p("vector", lambda e: e.tensor_scalar_mul(SCI, CMI, LGC))
        p("vector", lambda e: e.tensor_scalar_mul(MPA, DIFFA, LGC), (sV, 1))
        B.wait("scalar", sV)
        p("scalar", lambda e: e.activation(SCW, SCW, AF.Exp))
        p("scalar", lambda e: e.activation(SCI, SCI, AF.Exp))
        p("scalar", lambda e: e.activation(MPA, MPA, AF.Exp), (sS, 1))
        prologue_sS = sS.n

        def emit_pass_attn_a(src, dst):
            A = Region(arena, 0, ARENA_COLS, arena_bf)
            stage_f = [A.alloc(2048), A.alloc(2048)]
            stage_h = [A.alloc_bf(2048), A.alloc_bf(2048)]
            stage_l = [A.alloc_bf(2048), A.alloc_bf(2048)]
            B.wait("sync", sS, prologue_sS)
            split_sh = emit_split(
                [(d_wwr, D, D, d_wwrh, d_wwrl), (d_wrd, D, D, d_wrdh, d_wrdl)],
                stage_f, stage_h, stage_l)
            A.off = A.base
            WWRH = A.alloc_bf(DC * D)
            WWRL = A.alloc_bf(DC * D)
            WRDH = A.alloc_bf(DC * D)
            WRDL = A.alloc_bf(DC * D)
            UTHh = A.alloc_bf(D)
            UTLl = A.alloc_bf(D)
            RDTH = A.alloc_bf(D)
            RDTL = A.alloc_bf(D)
            WST = A.alloc(DC * D)
            XB = [A.alloc(D), A.alloc(D)]
            UBL = [A.alloc(D), A.alloc(D)]
            UTB = A.alloc(D)
            WKTB = A.alloc(D)
            WKBL = [A.alloc(D), A.alloc(D)]
            VB = A.alloc(D)
            VSB = A.alloc(D)
            SMB = A.alloc(128)
            RDS = A.alloc(D)
            LNSQ = A.alloc(D)
            FOUT0 = A.alloc(D)
            PREVR = A.alloc(D, parts=1)
            STASH = A.alloc(DC)
            STT = A.alloc(8)

            s1 = STT[:, 0:1]
            mean = STT[:, 1:2]
            var = STT[:, 2:3]
            rstd = STT[:, 3:4]
            ssq = STT[:, 5:6]
            sd = STT[:, 6:7]

            NTOT = NSEQ * NCH

            B.wait("sync", sSH, split_sh)
            for dstw, srcw in ((WWRH, d_wwrh), (WWRL, d_wwrl),
                               (WRDH, d_wrdh), (WRDL, d_wrdl)):
                p("sync", lambda e, dstw=dstw, srcw=srcw: e.dma_start(
                    out=dstw.rearrange("p (i e) -> p i e", i=DC),
                    in_=srcw.rearrange("(i p) e -> p i e", p=128)), (sLW, 16))
            w_ready = sLW.n

            ln_done = {}
            trans_done = {}
            shift_done = {-1: 0}
            wkt_dma = {-1: 0}
            rdt_evac = {-1: 0}
            wadd_done = {-1: 0}
            fin_done = {-1: 0}
            v_evac = {}
            wk_v = {}

            entry_sST = sST.n

            def emit_load(JJ):
                if JJ >= NTOT:
                    return
                r0 = JJ * CH
                xb = XB[JJ % 2]
                B.wait("sync", sV, fin_done.get(JJ - 2, 0))
                B.wait("sync", sST, entry_sST)
                p("sync", lambda e, r0=r0, xb=xb: e.dma_start(
                    out=xb, in_=src[r0:r0 + CH, :]), (sLX, 16))
                return sLX.n

            def emit_ln(JJ):
                if JJ >= NTOT:
                    return
                xb = XB[JJ % 2]
                ub = UBL[JJ % 2]
                B.wait("vector", sLX)
                p("vector", lambda e, xb=xb: e.reduce_sum(s1, xb, axis=AX.X), (sV, 1))
                B.wait("scalar", sLX)
                p("scalar", lambda e, xb=xb: e.activation(
                    LNSQ, xb, AF.Square, accum_out=ssq), (sS, 1))
                B.wait("vector", sV)
                B.wait("vector", sS)
                p("vector", lambda e: e.tensor_scalar_mul(mean, s1, col(C_NI1024)))
                p("vector", lambda e: e.tensor_scalar_mul(var, ssq, col(C_I1024)),
                  (sV, 1))
                B.wait("vector", sV)
                p("vector", lambda e: e.tensor_tensor(sd, mean, mean, op=ALU.mult),
                  (sV, 1))
                B.wait("vector", sV)
                p("vector", lambda e: e.tensor_tensor(var, var, sd, op=ALU.subtract),
                  (sV, 1))
                B.wait("scalar", sV)
                p("scalar", lambda e: e.activation(sd, var, AF.Sqrt, bias=col(C_EPS)),
                  (sS, 1))
                B.wait("vector", sS)
                p("vector", lambda e: e.reciprocal(rstd, sd), (sV, 1))
                B.wait("vector", sV)
                B.wait("vector", sT, trans_done.get(JJ - 2, 0))
                B.wait("vector", sSH, shift_done.get(JJ - 2, 0))
                p("vector", lambda e, xb=xb, ub=ub: e.tensor_scalar(
                    ub, xb, mean, rstd, op0=ALU.add, op1=ALU.mult), (sV, 1))
                ln_done[JJ] = sV.n

            def emit_shift(JJ):
                if JJ >= NTOT:
                    return
                Jloc = JJ % NCH
                ub = UBL[JJ % 2]
                wkb = WKBL[JJ % 2]
                B.wait("sync", sV, ln_done[JJ])
                p("sync", lambda e, ub=ub, wkb=wkb: e.dma_start(
                    out=wkb[1:128, :], in_=ub[0:127, :]), (sSH, 16))
                if Jloc == 0:
                    p("vector", lambda e, wkb=wkb: e.memset(wkb[0:1, :], 0.0), (sV, 1))
                else:
                    p("sync", lambda e, wkb=wkb: e.dma_start(out=wkb[0:1, :], in_=PREVR),
                      (sSH, 16))
                wk_sh = sSH.n
                wk_v[JJ] = sV.n
                B.wait("sync", sSH, wk_sh)
                p("sync", lambda e, ub=ub: e.dma_start(out=PREVR, in_=ub[127:128, :]),
                  (sSH, 16))
                shift_done[JJ] = sSH.n

            emit_load(0)
            emit_ln(0)
            emit_shift(0)
            emit_load(1)

            for JJ in range(NTOT):
                Jloc = JJ % NCH
                r0 = JJ * CH
                xb = XB[JJ % 2]
                ub = UBL[JJ % 2]
                wkb = WKBL[JJ % 2]

                B.wait("tensor", sV, ln_done[JJ])
                B.wait("tensor", sV, rdt_evac.get(JJ - 1, 0))
                B.wait("tensor", sSH, wkt_dma.get(JJ - 1, 0))
                for h in range(2):
                    pdst = (pT0 if h == 0 else pT1)
                    for i in range(4):
                        ii = h * 4 + i
                        p("tensor", lambda e, ii=ii, pdst=pdst, i=i, ub=ub: e.transpose(
                            pdst[:, i * 128:(i + 1) * 128],
                            ub[:, ii * 128:(ii + 1) * 128], IDN),
                          (sT, 1) if i == 3 else None)
                    B.wait("vector", sT)
                    p("vector", lambda e, h=h, pdst=pdst: e.tensor_copy(
                        UTB[:, h * 512:(h + 1) * 512], pdst))
                    p("vector", lambda e, h=h, pdst=pdst: e.tensor_copy(
                        UTHh[:, h * 512:(h + 1) * 512], pdst), (sV, 1))
                    B.wait("vector", sV)
                    p("vector", lambda e, h=h, pdst=pdst: e.tensor_tensor(
                        UTLl[:, h * 512:(h + 1) * 512], pdst,
                        UTHh[:, h * 512:(h + 1) * 512], op=ALU.subtract), (sV, 1))
                trans_done[JJ] = sT.n
                ut_done = sV.n

                B.wait("sync", sV, ut_done)
                p("sync", lambda e: e.dma_start(
                    out=WKTB.rearrange("p (i c) -> p i c", i=DC)[:, :, 1:128],
                    in_=UTB.rearrange("p (i c) -> p i c", i=DC)[:, :, 0:127]),
                  (sSH, 16))
                wkt_dma[JJ] = sSH.n
                if Jloc == 0:
                    p("vector", lambda e: e.memset(STASH, 0.0), (sV, 1))
                    B.wait("vector", sV)
                B.wait("vector", sSH, wkt_dma[JJ])
                p("vector", lambda e: e.tensor_copy(WKTB[:, 0:D:128], STASH),
                  (sV, 1))
                B.wait("vector", sV)
                p("vector", lambda e: e.tensor_copy(STASH, UTB[:, 127:D:128]),
                  (sV, 1))
                wkt_done = sV.n

                B.wait("tensor", sLW, w_ready)
                B.wait("tensor", sV, ut_done)
                for h in range(2):
                    for i in range(DC):
                        mm3(pA[:, h * 512:(h + 1) * 512],
                            UTHh[:, i * 128:(i + 1) * 128],
                            UTLl[:, i * 128:(i + 1) * 128],
                            WWRH[:, i * D + h * 512:i * D + (h + 1) * 512],
                            WWRL[:, i * D + h * 512:i * D + (h + 1) * 512],
                            start=(i == 0), stop=(i == DC - 1),
                            inc=(sT, 1) if (i == DC - 1 and h == 1) else None)
                B.wait("vector", sT)
                p("vector", lambda e: e.tensor_copy(VB, pA))
                p("vector", lambda e, Jloc=Jloc: e.tensor_scalar_mul(
                    VSB, pA, SCW[:, Jloc:Jloc + 1]), (sV, 1))
                v_evac[JJ] = sV.n

                B.wait("tensor", sV, wkt_done)
                for i in range(DC):
                    p("tensor", lambda e, i=i: e.matmul(
                        pT1[:, 0:128],
                        WKTB[:, i * 128:(i + 1) * 128],
                        UTB[:, i * 128:(i + 1) * 128],
                        start=(i == 0), stop=(i == DC - 1)),
                      (sT, 1) if i == DC - 1 else None)
                B.wait("vector", sT)
                p("vector", lambda e: e.tensor_tensor(
                    SMB, pT1[:, 0:128], MPA, op=ALU.mult), (sV, 1))
                sm_done = sV.n

                if Jloc > 0:
                    B.wait("tensor", sV, wadd_done[JJ - 1])
                    for h in range(2):
                        for i in range(DC):
                            p("tensor", lambda e, h=h, i=i: e.matmul(
                                pB[:, h * 512:(h + 1) * 512],
                                UTB[:, i * 128:(i + 1) * 128],
                                WST[:, i * D + h * 512:i * D + (h + 1) * 512],
                                start=(i == 0), stop=(i == DC - 1)))
                B.wait("tensor", sV, sm_done)
                for h in range(2):
                    p("tensor", lambda e, h=h: e.matmul(
                        pC[:, h * 512:(h + 1) * 512], SMB,
                        VB[:, h * 512:(h + 1) * 512],
                        start=True, stop=True),
                      (sT, 1) if h == 1 else None)
                B.wait("vector", sT)
                if Jloc > 0:
                    p("vector", lambda e, Jloc=Jloc: e.tensor_scalar_mul(
                        RDS, pB, SCI[:, Jloc:Jloc + 1]))
                    p("vector", lambda e: e.tensor_tensor(
                        RDS, RDS, pC, op=ALU.add), (sV, 1))
                else:
                    p("vector", lambda e: e.tensor_copy(RDS, pC), (sV, 1))
                rds_done = sV.n

                B.wait("tensor", sV, rds_done)
                for h in range(2):
                    pdst = (pT0 if h == 0 else pT1)
                    for i in range(4):
                        ii = h * 4 + i
                        p("tensor", lambda e, ii=ii, pdst=pdst, i=i: e.transpose(
                            pdst[:, i * 128:(i + 1) * 128],
                            RDS[:, ii * 128:(ii + 1) * 128], IDN),
                          (sT, 1) if i == 3 else None)
                    B.wait("vector", sT)
                    p("vector", lambda e, h=h, pdst=pdst: e.tensor_copy(
                        RDTH[:, h * 512:(h + 1) * 512], pdst), (sV, 1))
                    B.wait("vector", sV)
                    p("vector", lambda e, h=h, pdst=pdst: e.tensor_tensor(
                        RDTL[:, h * 512:(h + 1) * 512], pdst,
                        RDTH[:, h * 512:(h + 1) * 512], op=ALU.subtract), (sV, 1))
                rdt_evac[JJ] = sV.n

                B.wait("tensor", sV, rdt_evac[JJ])
                for h in range(2):
                    for i in range(DC):
                        mm3(pA[:, h * 512:(h + 1) * 512],
                            RDTH[:, i * 128:(i + 1) * 128],
                            RDTL[:, i * 128:(i + 1) * 128],
                            WRDH[:, i * D + h * 512:i * D + (h + 1) * 512],
                            WRDL[:, i * D + h * 512:i * D + (h + 1) * 512],
                            start=(i == 0), stop=(i == DC - 1),
                            inc=(sT, 1) if (i == DC - 1 and h == 1) else None)
                ro_done = sT.n

                emit_ln(JJ + 1)
                emit_shift(JJ + 1)

                B.wait("tensor", sSH, shift_done[JJ])
                B.wait("tensor", sV, max(wk_v.get(JJ, 0), v_evac[JJ], rds_done))
                wadd_base = None
                for j in range(DC):
                    pd = pB if (j % 2 == 0) else pC
                    if j >= 2:
                        B.wait("tensor", sV, wadd_base + (j - 2) + 1)
                    for h in range(2):
                        p("tensor", lambda e, j=j, h=h, pd=pd, wkb=wkb: e.matmul(
                            pd[:, h * 512:(h + 1) * 512],
                            wkb[:, j * 128:(j + 1) * 128],
                            VSB[:, h * 512:(h + 1) * 512],
                            start=True, stop=True),
                          (sT, 1) if h == 1 else None)
                    if j == 0:
                        wadd_base = sV.n
                    B.wait("vector", sT)
                    if Jloc == 0:
                        p("vector", lambda e, j=j, pd=pd: e.tensor_copy(
                            WST[:, j * D:(j + 1) * D], pd), (sV, 1))
                    else:
                        p("vector", lambda e, j=j, pd=pd: e.tensor_tensor(
                            WST[:, j * D:(j + 1) * D],
                            WST[:, j * D:(j + 1) * D], pd, op=ALU.add), (sV, 1))
                wadd_done[JJ] = sV.n

                B.wait("vector", sT, ro_done)
                B.wait("vector", sST)
                p("vector", lambda e, ub=ub: e.scalar_tensor_tensor(
                    FOUT0, pA, ALC, ub, op0=ALU.mult, op1=ALU.add))
                p("vector", lambda e, xb=xb: e.tensor_tensor(
                    FOUT0, FOUT0, xb, op=ALU.add), (sV, 1))
                fin_done[JJ] = sV.n

                B.wait("gpsimd", sV, fin_done[JJ])
                p("gpsimd", lambda e, r0=r0, dst=dst: e.dma_start(
                    out=dst[r0:r0 + CH, :], in_=FOUT0), (sST, 16))

                emit_load(JJ + 2)

        GRP = 4
        NGRP = NCH // GRP

        def mm3(pz, sh, sl, mh, ml, start, stop, inc=None):
            p("tensor", lambda e: e.matmul(pz, sh, mh, start=start, stop=False))
            p("tensor", lambda e: e.matmul(pz, sh, ml, start=False, stop=False))
            B.em("tensor", lambda e: e.matmul(pz, sl, mh, start=False, stop=stop),
                 inc)

        def emit_split(jobs, stage_f, stage_h, stage_l):
            cast_done = {}
            store_done = {}
            i = 0
            for dw, Rw, Cw, dh, dl in jobs:
                csplit = 1 if Cw <= 2048 else Cw // 2048
                cw = Cw // csplit
                g = max(1, min(Rw // 128, 2048 // cw))
                w3 = dw.rearrange("(n p) c -> p n c", p=128)
                h3 = dh.rearrange("(n p) c -> p n c", p=128)
                l3 = dl.rearrange("(n p) c -> p n c", p=128)
                for cs in range(csplit):
                    for k0 in range(0, Rw // 128, g):
                        q = i % 2
                        W = g * cw
                        B.wait("sync", sV, cast_done.get(i - 2, 0))
                        p("sync", lambda e, q=q, W=W, k0=k0, g=g, cs=cs, cw=cw,
                          w3=w3: e.dma_start(
                              out=stage_f[q][:, 0:W].rearrange(
                                  "p (n c) -> p n c", n=g),
                              in_=w3[:, k0:k0 + g, cs * cw:(cs + 1) * cw]),
                          (sLW, 16))
                        B.wait("vector", sLW)
                        B.wait("vector", sSH, store_done.get(i - 2, 0))
                        p("vector", lambda e, q=q, W=W: e.tensor_copy(
                            stage_h[q][:, 0:W], stage_f[q][:, 0:W]), (sV, 1))
                        B.wait("vector", sV)
                        p("vector", lambda e, q=q, W=W: e.tensor_tensor(
                            stage_l[q][:, 0:W], stage_f[q][:, 0:W],
                            stage_h[q][:, 0:W], op=ALU.subtract), (sV, 1))
                        cast_done[i] = sV.n
                        B.wait("sync", sV, cast_done[i])
                        for st, d3 in ((stage_h, h3), (stage_l, l3)):
                            p("sync", lambda e, q=q, W=W, k0=k0, g=g, cs=cs,
                              cw=cw, st=st, d3=d3: e.dma_start(
                                  out=d3[:, k0:k0 + g, cs * cw:(cs + 1) * cw],
                                  in_=st[q][:, 0:W].rearrange(
                                      "p (n c) -> p n c", n=g)),
                              (sSH, 16))
                        store_done[i] = sSH.n
                        i += 1
            return sSH.n

        ffn_split_state = {}

        def emit_pass_ffn(src, dst):
            A = Region(arena, 0, ARENA_COLS, arena_bf)
            if "done" not in ffn_split_state:
                stage_f = [A.alloc(2048), A.alloc(2048)]
                stage_h = [A.alloc_bf(2048), A.alloc_bf(2048)]
                stage_l = [A.alloc_bf(2048), A.alloc_bf(2048)]
                B.wait("sync", sV)
                B.wait("sync", sT)
                B.wait("sync", sS)
                B.wait("vector", sST)
                B.wait("scalar", sST)
                sh_done = emit_split(
                    [(d_proj, D, E16, d_pjh, d_pjl),
                     (d_win1, D, R, d_w1h, d_w1l),
                     (d_wout1, R, O, d_wo1h, d_wo1l),
                     (d_win2, O, R, d_w2h, d_w2l),
                     (d_wout2, R, D, d_wt2h, d_wt2l)],
                    stage_f, stage_h, stage_l)
                ffn_split_state["done"] = sh_done
                A.off = A.base
            split_sh = ffn_split_state["done"]

            WIN1H = A.alloc_bf(DC * R)
            WIN1L = A.alloc_bf(DC * R)
            WOUT2H = A.alloc_bf(4 * D)
            WOUT2L = A.alloc_bf(4 * D)
            PROJH = A.alloc_bf(DC * E16)
            PROJL = A.alloc_bf(DC * E16)
            UTH = A.alloc_bf(D)
            UTL = A.alloc_bf(D)
            Z1TH = A.alloc_bf(4 * 512)
            Z1TL = A.alloc_bf(4 * 512)
            WO1BH = [A.alloc_bf(4 * 128) for _ in range(3)]
            WO1BL = [A.alloc_bf(4 * 128) for _ in range(3)]
            W2SH = [A.alloc_bf(R) for _ in range(3)]
            W2SL = [A.alloc_bf(R) for _ in range(3)]
            HTH = [A.alloc_bf(R), A.alloc_bf(R)]
            HTL = [A.alloc_bf(R), A.alloc_bf(R)]
            Z2TH = A.alloc_bf(R)
            Z2TL = A.alloc_bf(R)
            C12 = A.alloc(2 * R, parts=16)
            W1B = A.alloc(R, parts=1)
            W2B = A.alloc(R, parts=1)
            U16S = A.alloc(E16, parts=16)
            E64C = A.alloc(3 * 64)
            XGL = [[A.alloc(D) for _ in range(GRP)], [A.alloc(D) for _ in range(GRP)]]
            UB4 = [A.alloc(D) for _ in range(GRP)]
            LNSQF = A.alloc(D)
            Z1F = [A.alloc(R) for _ in range(GRP)]
            PRAW = A.alloc(64)
            CEN = A.alloc(64)
            PALL = A.alloc(64)
            SRT = [A.alloc(64), A.alloc(64)]
            SQ64 = A.alloc(64)
            CS64 = A.alloc(64)
            CSQ64 = A.alloc(64)
            MEAN64 = A.alloc(64)
            WRK64 = A.alloc(64)
            TAU64 = A.alloc(64)
            AOUT = A.alloc(64)
            ST4 = A.alloc(16)
            ATG = A.alloc(GRP * 128, parts=16)
            ZST = A.alloc(128, parts=16)
            ZSQT = A.alloc(128, parts=16)
            HTB = [A.alloc(R), A.alloc(R)]
            GB = A.alloc(R)
            Z2S = A.alloc(R)
            FOUT = A.alloc(D)

            nm4 = ST4[:, 0:4]
            v4 = ST4[:, 4:8]
            sd4 = ST4[:, 8:12]
            supp4 = ST4[:, 12:16]
            IOTA64 = E64C[:, 0:64]
            KINV64 = E64C[:, 64:128]
            KVAL64 = E64C[:, 128:192]

            def seg(ap64):
                return ap64.rearrange("p (s e) -> p s e", e=E16)

            def bc(ap64seg, ap4):
                return bass.broadcast_tensor_aps(
                    ap64seg, ap4.rearrange("p (s one) -> p s one", one=1))

            B.wait("sync", sV)
            B.wait("sync", sT)
            B.wait("sync", sS)
            B.wait("sync", sSH, split_sh)
            B.wait("vector", sST)
            B.wait("scalar", sST)
            for dstw, srcw in ((WIN1H, d_w1h), (WIN1L, d_w1l),
                               (WOUT2H, d_wt2h), (WOUT2L, d_wt2l),
                               (PROJH, d_pjh), (PROJL, d_pjl)):
                nrows = srcw.shape[0] // 128
                p("sync", lambda e, dstw=dstw, srcw=srcw, nrows=nrows: e.dma_start(
                    out=dstw.rearrange("p (i r) -> p i r", i=nrows),
                    in_=srcw.rearrange("(i p) r -> p i r", p=128)), (sLW, 16))
            p("sync", lambda e: e.dma_start(out=C12, in_=d_c1c2), (sLW, 16))
            p("sync", lambda e: e.dma_start(out=W1B, in_=d_w1b), (sLW, 16))
            p("sync", lambda e: e.dma_start(out=W2B, in_=d_w2b), (sLW, 16))
            p("sync", lambda e: e.dma_start(out=U16S, in_=d_u16), (sLW, 16))
            p("sync", lambda e: e.dma_start(out=E64C, in_=d_e64c), (sLW, 16))
            w_ready = sLW.n
            entry_sST = sST.n

            NGRPT = NSEQ * NGRP
            ln_done = {}
            fin_done = {}
            trans_done = {}
            prev_ht_tail = [0]

            def vg(fn):
                p("vector", fn, (sV, 1))
                B.wait("vector", sV)

            def emit_lnblock(gg):
                if gg >= NGRPT:
                    return
                xg = XGL[gg % 2]
                for c in range(GRP):
                    JJ = gg * GRP + c
                    r0 = JJ * CH
                    B.wait("gpsimd", sV, fin_done.get((gg - 2, c), 0))
                    p("gpsimd", lambda e, r0=r0, xb=xg[c]: e.dma_start(
                        out=xb, in_=src[r0:r0 + CH, :]), (sLX, 16))
                B.wait("vector", sLX)
                B.wait("scalar", sLX)
                B.wait("vector", sT, trans_done.get(gg - 1, 0))
                for c in range(GRP):
                    p("vector", lambda e, c=c, xg=xg: e.reduce_sum(
                        nm4[:, c:c + 1], xg[c], axis=AX.X),
                      (sV, 1) if c == GRP - 1 else None)
                    p("scalar", lambda e, c=c, xg=xg: e.activation(
                        LNSQF, xg[c], AF.Square, accum_out=v4[:, c:c + 1]),
                      (sS, 1) if c == GRP - 1 else None)
                B.wait("vector", sV)
                B.wait("vector", sS)
                vg(lambda e: e.tensor_scalar_mul(nm4, nm4, col(C_NI1024)))
                vg(lambda e: e.tensor_scalar_mul(v4, v4, col(C_I1024)))
                vg(lambda e: e.tensor_tensor(sd4, nm4, nm4, op=ALU.mult))
                vg(lambda e: e.tensor_tensor(v4, v4, sd4, op=ALU.subtract))
                B.wait("scalar", sV)
                p("scalar", lambda e: e.activation(sd4, v4, AF.Sqrt, bias=col(C_EPS)),
                  (sS, 1))
                B.wait("vector", sS)
                vg(lambda e: e.reciprocal(sd4, sd4))
                for c in range(GRP):
                    p("vector", lambda e, c=c, xg=xg: e.tensor_scalar(
                        UB4[c], xg[c], nm4[:, c:c + 1], sd4[:, c:c + 1],
                        op0=ALU.add, op1=ALU.mult),
                      (sV, 1) if c == GRP - 1 else None)
                ln_done[gg] = sV.n

            emit_lnblock(0)

            for gg in range(NGRPT):
                g = gg % NGRP
                xg = XGL[gg % 2]

                B.wait("tensor", sV, ln_done[gg])
                B.wait("tensor", sV)
                B.wait("tensor", sS)
                B.wait("tensor", sLW, w_ready)
                for c in range(GRP):
                    for h in range(2):
                        pdst = (pT0 if h == 0 else pT1)
                        for i in range(4):
                            ii = h * 4 + i
                            p("tensor", lambda e, ii=ii, pdst=pdst, i=i, c=c:
                              e.transpose(pdst[:, i * 128:(i + 1) * 128],
                                          UB4[c][:, ii * 128:(ii + 1) * 128], IDN),
                              (sT, 1) if i == 3 else None)
                        B.wait("vector", sT)
                        p("vector", lambda e, h=h, pdst=pdst: e.tensor_copy(
                            UTH[:, h * 512:(h + 1) * 512], pdst), (sV, 1))
                        B.wait("vector", sV)
                        p("vector", lambda e, h=h, pdst=pdst: e.tensor_tensor(
                            UTL[:, h * 512:(h + 1) * 512], pdst,
                            UTH[:, h * 512:(h + 1) * 512], op=ALU.subtract),
                          (sV, 1))
                    ut_done = sV.n
                    B.wait("tensor", sV, ut_done)
                    for i in range(DC):
                        mm3(pA[:, 0:16],
                            UTH[:, i * 128:(i + 1) * 128],
                            UTL[:, i * 128:(i + 1) * 128],
                            PROJH[:, i * E16:(i + 1) * E16],
                            PROJL[:, i * E16:(i + 1) * E16],
                            start=(i == 0), stop=(i == DC - 1))
                    for i in range(DC):
                        mm3(pA[:, 512:1024],
                            UTH[:, i * 128:(i + 1) * 128],
                            UTL[:, i * 128:(i + 1) * 128],
                            WIN1H[:, i * R:(i + 1) * R],
                            WIN1L[:, i * R:(i + 1) * R],
                            start=(i == 0), stop=False)
                    p("tensor", lambda e: e.matmul(
                        pA[:, 512:1024], ONESR, W1B, start=False, stop=True),
                      (sT, 1))
                    B.wait("vector", sT)
                    p("vector", lambda e, c=c: e.tensor_copy(Z1F[c], pA[:, 512:1024]))
                    p("vector", lambda e, c=c: e.tensor_copy(
                        PRAW[:, c * E16:(c + 1) * E16], pA[:, 0:16]), (sV, 1))
                trans_done[gg] = sT.n
                praw_done = sV.n

                B.wait("vector", sV, praw_done)
                vg(lambda e: e.reduce_sum(nm4, seg(PRAW), axis=AX.X))
                vg(lambda e: e.tensor_scalar_mul(nm4, nm4, col(C_NI16)))
                _pa, _pb = bc(seg(PRAW), nm4)
                p("vector", lambda e, _pa=_pa, _pb=_pb: e.tensor_tensor(
                    seg(CEN), _pa, _pb, op=ALU.add), (sV, 1))
                B.wait("vector", sV)
                vg(lambda e: e.tensor_tensor(WRK64, CEN, CEN, op=ALU.mult))
                vg(lambda e: e.reduce_sum(v4, seg(WRK64), axis=AX.X))
                vg(lambda e: e.tensor_scalar_mul(v4, v4, col(C_I16)))
                B.wait("scalar", sV)
                p("scalar", lambda e: e.activation(sd4, v4, AF.Sqrt, bias=col(C_EPS)),
                  (sS, 1))
                B.wait("vector", sS)
                vg(lambda e: e.reciprocal(sd4, sd4))
                vg(lambda e: e.tensor_scalar_mul(sd4, sd4, col(C_HALF)))
                _ca, _cb = bc(seg(CEN), sd4)
                p("vector", lambda e, _ca=_ca, _cb=_cb: e.tensor_tensor(
                    seg(PALL), _ca, _cb, op=ALU.mult), (sV, 1))
                B.wait("vector", sV)
                vg(lambda e: e.reduce_max(supp4, seg(PALL), axis=AX.X))
                _za, _zb = bc(seg(PALL), supp4)
                p("vector", lambda e, _za=_za, _zb=_zb: e.tensor_tensor(
                    seg(PALL), _za, _zb, op=ALU.subtract), (sV, 1))
                B.wait("vector", sV)
                cur, nxt = 0, 1
                vg(lambda e: e.tensor_copy(SRT[0], PALL))
                for rnd in range(16):
                    a3 = SRT[cur].rearrange("p (s e) -> p s e", e=E16)
                    b3 = SRT[nxt].rearrange("p (s e) -> p s e", e=E16)
                    if rnd % 2 == 0:
                        av = a3.rearrange("p s (q two) -> p s q two", two=2)
                        bv = b3.rearrange("p s (q two) -> p s q two", two=2)
                        p("vector", lambda e, av=av, bv=bv: e.tensor_tensor(
                            bv[:, :, :, 0], av[:, :, :, 0], av[:, :, :, 1],
                            op=ALU.max))
                        p("vector", lambda e, av=av, bv=bv: e.tensor_tensor(
                            bv[:, :, :, 1], av[:, :, :, 0], av[:, :, :, 1],
                            op=ALU.min), (sV, 1))
                        B.wait("vector", sV)
                    else:
                        am = a3[:, :, 1:15].rearrange("p s (q two) -> p s q two",
                                                      two=2)
                        bm = b3[:, :, 1:15].rearrange("p s (q two) -> p s q two",
                                                      two=2)
                        p("vector", lambda e, am=am, bm=bm: e.tensor_tensor(
                            bm[:, :, :, 0], am[:, :, :, 0], am[:, :, :, 1],
                            op=ALU.max))
                        p("vector", lambda e, am=am, bm=bm: e.tensor_tensor(
                            bm[:, :, :, 1], am[:, :, :, 0], am[:, :, :, 1],
                            op=ALU.min))
                        p("vector", lambda e, a3=a3, b3=b3: e.tensor_copy(
                            b3[:, :, 0:1], a3[:, :, 0:1]))
                        p("vector", lambda e, a3=a3, b3=b3: e.tensor_copy(
                            b3[:, :, 15:16], a3[:, :, 15:16]), (sV, 1))
                        B.wait("vector", sV)
                    cur, nxt = nxt, cur
                ZS = SRT[cur]
                B.wait("scalar", sV)
                p("scalar", lambda e: e.activation(SQ64, ZS, AF.Square), (sS, 1))
                sq_done = sS.n

                for c in range(GRP):
                    zc = ZS[:, c * E16:(c + 1) * E16]
                    B.wait("tensor", sV)
                    B.wait("tensor", sS, sq_done)
                    p("tensor", lambda e, zc=zc: e.transpose(
                        pT0[0:16, 0:128], zc, IDN))
                    p("tensor", lambda e, c=c: e.transpose(
                        pT0[0:16, 128:256], SQ64[:, c * E16:(c + 1) * E16], IDN),
                      (sT, 1))
                    B.wait("vector", sT)
                    p("vector", lambda e: e.tensor_copy(ZST, pT0[0:16, 0:128]))
                    p("vector", lambda e: e.tensor_copy(ZSQT, pT0[0:16, 128:256]),
                      (sV, 1))
                    B.wait("tensor", sV)
                    p("tensor", lambda e: e.matmul(
                        pT1[:, 0:16], ZST, U16S, start=True, stop=True))
                    p("tensor", lambda e: e.matmul(
                        pT1[:, 16:32], ZSQT, U16S, start=True, stop=True), (sT, 1))
                    B.wait("vector", sT)
                    p("vector", lambda e, c=c: e.tensor_copy(
                        CS64[:, c * E16:(c + 1) * E16], pT1[:, 0:16]))
                    p("vector", lambda e, c=c: e.tensor_copy(
                        CSQ64[:, c * E16:(c + 1) * E16], pT1[:, 16:32]), (sV, 1))
                cs_done = sV.n

                B.wait("vector", sV, cs_done)
                vg(lambda e: e.tensor_tensor(MEAN64, CS64, KINV64, op=ALU.mult))
                vg(lambda e: e.tensor_tensor(WRK64, CSQ64, KINV64, op=ALU.mult))
                vg(lambda e: e.tensor_tensor(TAU64, MEAN64, MEAN64, op=ALU.mult))
                vg(lambda e: e.tensor_tensor(WRK64, WRK64, TAU64, op=ALU.subtract))
                vg(lambda e: e.tensor_tensor(WRK64, WRK64, KVAL64, op=ALU.mult))
                vg(lambda e: e.tensor_scalar(WRK64, WRK64, col(C_NONE), col(C_ONE),
                                             op0=ALU.mult, op1=ALU.add))
                vg(lambda e: e.tensor_tensor(WRK64, WRK64, KINV64, op=ALU.mult))
                vg(lambda e: e.tensor_scalar_max(WRK64, WRK64, col(C_ZERO)))
                B.wait("scalar", sV)
                p("scalar", lambda e: e.activation(WRK64, WRK64, AF.Sqrt), (sS, 1))
                B.wait("vector", sS)
                vg(lambda e: e.tensor_tensor(TAU64, MEAN64, WRK64, op=ALU.subtract))
                vg(lambda e: e.tensor_tensor(WRK64, TAU64, ZS, op=ALU.is_le))
                vg(lambda e: e.reduce_sum(supp4, seg(WRK64), axis=AX.X))
                vg(lambda e: e.tensor_scalar_add(supp4, supp4, col(C_NONE)))
                _ia, _ib = bc(seg(IOTA64), supp4)
                p("vector", lambda e, _ia=_ia, _ib=_ib: e.tensor_tensor(
                    seg(WRK64), _ia, _ib, op=ALU.is_equal), (sV, 1))
                B.wait("vector", sV)
                vg(lambda e: e.tensor_tensor(WRK64, WRK64, TAU64, op=ALU.mult))
                vg(lambda e: e.reduce_sum(supp4, seg(WRK64), axis=AX.X))
                _qa, _qb = bc(seg(PALL), supp4)
                p("vector", lambda e, _qa=_qa, _qb=_qb: e.tensor_tensor(
                    seg(AOUT), _qa, _qb, op=ALU.subtract), (sV, 1))
                B.wait("vector", sV)
                vg(lambda e: e.tensor_scalar_max(AOUT, AOUT, col(C_ZERO)))
                vg(lambda e: e.tensor_tensor(AOUT, AOUT, AOUT, op=ALU.mult))

                for c in range(GRP):
                    B.wait("tensor", sV)
                    p("tensor", lambda e, c=c: e.transpose(
                        pT0[0:16, 0:128], AOUT[:, c * E16:(c + 1) * E16], IDN),
                      (sT, 1))
                    B.wait("vector", sT)
                    p("vector", lambda e, c=c: e.tensor_copy(
                        ATG[:, c * 128:(c + 1) * 128], pT0[0:16, 0:128]), (sV, 1))
                    atg_done = sV.n
                    B.wait("tensor", sV, atg_done)
                    p("tensor", lambda e, c=c: e.matmul(
                        pB[:, 0:512], ATG[:, c * 128:(c + 1) * 128],
                        C12[:, 0:512], start=True, stop=True), (sT, 1))
                    B.wait("vector", sT)
                    p("vector", lambda e: e.tensor_copy(GB, pB[:, 0:512]), (sV, 1))
                    B.wait("vector", sV)
                    p("vector", lambda e, c=c: e.tensor_tensor(
                        Z1F[c], Z1F[c], GB, op=ALU.mult), (sV, 1))
                    z1_done = sV.n
                    B.wait("tensor", sV, z1_done)
                    for i in range(4):
                        p("tensor", lambda e, i=i, c=c: e.transpose(
                            pT0[:, i * 128:(i + 1) * 128],
                            Z1F[c][:, i * 128:(i + 1) * 128], IDN),
                          (sT, 1) if i == 3 else None)
                    B.wait("vector", sT)
                    p("vector", lambda e, c=c: e.tensor_copy(
                        Z1TH.rearrange("p (i t) -> p i t", i=4)[:, :,
                                                               c * 128:(c + 1) * 128],
                        pT0[:, 0:512].rearrange("p (i t) -> p i t", i=4)), (sV, 1))
                    B.wait("vector", sV)
                    p("vector", lambda e, c=c: e.tensor_tensor(
                        Z1TL.rearrange("p (i t) -> p i t", i=4)[:, :,
                                                               c * 128:(c + 1) * 128],
                        pT0[:, 0:512].rearrange("p (i t) -> p i t", i=4),
                        Z1TH.rearrange("p (i t) -> p i t", i=4)[:, :,
                                                                c * 128:(c + 1) * 128],
                        op=ALU.subtract), (sV, 1))

                gelu_dones = {}
                ht_dones = {}
                z2_dones = {}
                htsplit = {}
                NOC = O // 128

                dma_done = {}

                def emit_stream(oc):
                    if oc >= NOC:
                        return
                    buf = oc % 3
                    if oc >= 3:
                        B.wait("sync", sT, ht_dones[oc - 3])
                        B.wait("sync", sT, z2_dones[oc - 3])
                    else:
                        B.wait("sync", sT, prev_ht_tail[0])
                    p("sync", lambda e, oc=oc, buf=buf: e.dma_start(
                        out=WO1BH[buf].rearrange("p (i o) -> p i o", i=4),
                        in_=d_wo1h[:, oc * 128:(oc + 1) * 128].rearrange(
                            "(i p) o -> p i o", p=128)), (sWO, 16))
                    p("sync", lambda e, oc=oc, buf=buf: e.dma_start(
                        out=WO1BL[buf].rearrange("p (i o) -> p i o", i=4),
                        in_=d_wo1l[:, oc * 128:(oc + 1) * 128].rearrange(
                            "(i p) o -> p i o", p=128)), (sWO, 16))
                    p("sync", lambda e, oc=oc, buf=buf: e.dma_start(
                        out=W2SH[buf], in_=d_w2h[oc * 128:(oc + 1) * 128, :]),
                      (sWO, 16))
                    p("sync", lambda e, oc=oc, buf=buf: e.dma_start(
                        out=W2SL[buf], in_=d_w2l[oc * 128:(oc + 1) * 128, :]),
                      (sWO, 16))
                    dma_done[oc] = sWO.n

                def emit_ht(oc):
                    buf = oc % 3
                    B.wait("tensor", sV)
                    B.wait("tensor", sWO, dma_done[oc])
                    if oc >= 2:
                        B.wait("tensor", sS, gelu_dones[oc - 2])
                    bank = pC[:, (oc % 2) * 512:(oc % 2 + 1) * 512]
                    for i in range(4):
                        mm3(bank,
                            WO1BH[buf][:, i * 128:(i + 1) * 128],
                            WO1BL[buf][:, i * 128:(i + 1) * 128],
                            Z1TH[:, i * 512:(i + 1) * 512],
                            Z1TL[:, i * 512:(i + 1) * 512],
                            start=(i == 0), stop=(i == 3),
                            inc=(sT, 1) if i == 3 else None)
                    ht_dones[oc] = sT.n
                    hbuf = oc % 2
                    B.wait("scalar", sT, ht_dones[oc])
                    if oc >= 2:
                        B.wait("scalar", sT, z2_dones[oc - 2])
                    p("scalar", lambda e, hbuf=hbuf, bank=bank: e.activation(
                        HTB[hbuf], bank, AF.Gelu), (sS, 1))
                    gelu_dones[oc] = sS.n
                    B.wait("vector", sS, gelu_dones[oc])
                    if oc >= 2:
                        B.wait("vector", sT, z2_dones[oc - 2])
                    p("vector", lambda e, hbuf=hbuf: e.tensor_copy(HTH[hbuf], HTB[hbuf]),
                      (sV, 1))
                    B.wait("vector", sV)
                    p("vector", lambda e, hbuf=hbuf: e.tensor_tensor(
                        HTL[hbuf], HTB[hbuf], HTH[hbuf], op=ALU.subtract), (sV, 1))
                    htsplit[oc] = sV.n

                def emit_z2(oc):
                    buf = oc % 2
                    sbuf = oc % 3
                    B.wait("tensor", sV, htsplit[oc])
                    for c in range(GRP):
                        pz = (pA if c < 2 else pB)[:, (c % 2) * 512:(c % 2 + 1) * 512]
                        mm3(pz,
                            HTH[buf][:, c * 128:(c + 1) * 128],
                            HTL[buf][:, c * 128:(c + 1) * 128],
                            W2SH[sbuf], W2SL[sbuf],
                            start=(oc == 0), stop=False,
                            inc=(sT, 1) if c == GRP - 1 else None)
                    z2_dones[oc] = sT.n

                emit_stream(0)
                emit_stream(1)
                emit_stream(2)
                emit_ht(0)
                emit_ht(1)
                emit_z2(0)
                emit_stream(3)
                emit_ht(2)
                emit_z2(1)
                emit_stream(4)
                emit_lnblock(gg + 1)
                for oc in range(3, NOC):
                    emit_ht(oc)
                    emit_z2(oc - 1)
                    emit_stream(oc + 2)
                emit_z2(NOC - 1)
                prev_ht_tail[0] = ht_dones[NOC - 1]
                for c in range(GRP):
                    pz = (pA if c < 2 else pB)[:, (c % 2) * 512:(c % 2 + 1) * 512]
                    p("tensor", lambda e, pz=pz: e.matmul(
                        pz, ONESR, W2B, start=False, stop=True),
                      (sT, 1) if c == GRP - 1 else None)
                z2_done = sT.n

                g2_done = {}
                for c in range(GRP):
                    B.wait("tensor", sV)
                    p("tensor", lambda e, c=c: e.matmul(
                        pT1[:, 0:512], ATG[:, c * 128:(c + 1) * 128],
                        C12[:, 512:1024], start=True, stop=True), (sT, 1))
                    g2_done[c] = sT.n
                    pz = (pA if c < 2 else pB)[:, (c % 2) * 512:(c % 2 + 1) * 512]
                    B.wait("vector", sT, g2_done[c])
                    B.wait("vector", sT, z2_done)
                    p("vector", lambda e: e.tensor_copy(GB, pT1[:, 0:512]), (sV, 1))
                    B.wait("vector", sV)
                    p("vector", lambda e, pz=pz: e.tensor_tensor(
                        Z2S, pz, GB, op=ALU.mult), (sV, 1))
                    z2f_done = sV.n
                    B.wait("tensor", sV, z2f_done)
                    for i in range(4):
                        p("tensor", lambda e, i=i: e.transpose(
                            pT0[:, i * 128:(i + 1) * 128],
                            Z2S[:, i * 128:(i + 1) * 128], IDN),
                          (sT, 1) if i == 3 else None)
                    B.wait("vector", sT)
                    p("vector", lambda e: e.tensor_copy(Z2TH, pT0[:, 0:512]), (sV, 1))
                    B.wait("vector", sV)
                    p("vector", lambda e: e.tensor_tensor(
                        Z2TL, pT0[:, 0:512], Z2TH, op=ALU.subtract), (sV, 1))
                    z2t_done = sV.n
                    B.wait("tensor", sV, z2t_done)
                    B.wait("tensor", sS)
                    for h in range(2):
                        for i in range(4):
                            mm3(pC[:, h * 512:(h + 1) * 512],
                                Z2TH[:, i * 128:(i + 1) * 128],
                                Z2TL[:, i * 128:(i + 1) * 128],
                                WOUT2H[:, i * D + h * 512:i * D + (h + 1) * 512],
                                WOUT2L[:, i * D + h * 512:i * D + (h + 1) * 512],
                                start=(i == 0), stop=(i == 3),
                                inc=(sT, 1) if (i == 3 and h == 1) else None)
                    oo_done = sT.n
                    B.wait("vector", sT, oo_done)
                    B.wait("vector", sST)
                    p("vector", lambda e, c=c, xg=xg: e.tensor_tensor(
                        FOUT, pC, xg[c], op=ALU.add), (sV, 1))
                    fin_done[(gg, c)] = sV.n
                    JJ = gg * GRP + c
                    r0 = JJ * CH
                    B.wait("gpsimd", sV, fin_done[(gg, c)])
                    p("gpsimd", lambda e, r0=r0, dst=dst: e.dma_start(
                        out=dst[r0:r0 + CH, :], in_=FOUT), (sST, 16))

        def emit_pass_attn_b(src, dst):
            A = Region(arena, 0, ARENA_COLS, arena_bf)
            stage_f = [A.alloc(2048), A.alloc(2048)]
            stage_h = [A.alloc_bf(2048), A.alloc_bf(2048)]
            stage_l = [A.alloc_bf(2048), A.alloc_bf(2048)]
            B.wait("sync", sV)
            B.wait("sync", sT)
            B.wait("sync", sS)
            B.wait("vector", sST)
            B.wait("scalar", sST)
            split_sh = emit_split(
                [(d_wq, D, D, d_qh, d_ql), (d_wk, D, D, d_kh, d_kl),
                 (d_wv, D, D, d_vh, d_vl), (d_wo, D, D, d_oh, d_ol),
                 (d_wg1, D, LR, d_g1hh, d_g1ll), (d_wg2, LR, D, d_g2hh, d_g2ll)],
                stage_f, stage_h, stage_l)
            A.off = A.base

            WQH = A.alloc_bf(DC * D)
            WQL = A.alloc_bf(DC * D)
            WKH = A.alloc_bf(DC * D)
            WKL = A.alloc_bf(DC * D)
            WVH = A.alloc_bf(DC * D)
            WVL = A.alloc_bf(DC * D)
            WOH = A.alloc_bf(DC * D)
            WOL = A.alloc_bf(DC * D)
            WG1H = A.alloc_bf(DC * LR)
            WG1L = A.alloc_bf(DC * LR)
            WG2H = A.alloc_bf(D)
            WG2L = A.alloc_bf(D)
            UTH = A.alloc_bf(D)
            UTL = A.alloc_bf(D)
            G1TH = A.alloc_bf(LR)
            G1TL = A.alloc_bf(LR)
            TRILS = A.alloc(128)
            TRILB = A.alloc_bf(D)
            XB2L = [A.alloc(D), A.alloc(D)]
            UB2 = A.alloc(D)
            QB = A.alloc(D)
            eb_off = A.off
            EB = A.alloc(D)
            VB2 = A.alloc(D)
            PB2 = A.alloc(D)
            PT2 = A.alloc(D)
            ET2 = A.alloc(D)
            SCM = A.alloc(D)
            STB = A.alloc(D)
            GSB = A.alloc(D)
            OB = A.alloc(D)
            AB2 = OB
            CARRY = A.alloc(D, parts=1)
            STT = A.alloc(8)
            s1 = STT[:, 0:1]
            mean = STT[:, 1:2]
            var = STT[:, 2:3]
            rstd = STT[:, 3:4]
            sd = STT[:, 4:5]
            ssq = STT[:, 5:6]
            LNSQ2 = QB
            FOUT2 = VB2
            G1B = SCM[:, 0:LR]
            OTH = arena_bf[0:128, 2 * eb_off:2 * eb_off + D]
            OTL = arena_bf[0:128, 2 * eb_off + D:2 * eb_off + 2 * D]

            B.wait("sync", sSH, split_sh)
            for dstw, srcw in ((WQH, d_qh), (WQL, d_ql), (WKH, d_kh), (WKL, d_kl),
                               (WVH, d_vh), (WVL, d_vl), (WOH, d_oh), (WOL, d_ol),
                               (WG1H, d_g1hh), (WG1L, d_g1ll)):
                nrows = srcw.shape[0] // 128
                p("sync", lambda e, dstw=dstw, srcw=srcw, nrows=nrows: e.dma_start(
                    out=dstw.rearrange("p (i r) -> p i r", i=nrows),
                    in_=srcw.rearrange("(i p) r -> p i r", p=128)), (sLW, 16))
            p("sync", lambda e: e.dma_start(out=WG2H, in_=d_g2hh), (sLW, 16))
            p("sync", lambda e: e.dma_start(out=WG2L, in_=d_g2ll), (sLW, 16))
            p("sync", lambda e: e.dma_start(out=TRILS, in_=d_tril), (sLW, 16))
            p("sync", lambda e: e.dma_start(out=TRILB, in_=d_trilb), (sLW, 16))
            w_ready = sLW.n
            entry_sST = sST.n
            B.wait("vector", sST, entry_sST)
            B.wait("scalar", sST, entry_sST)

            NTOT = NSEQ * NCH
            ln_done = {}
            trans_done = {}
            fin_done = {}
            p_done = {}
            oo_done = {}
            ot_evac = {-1: 0}

            def emit_load(JJ):
                if JJ >= NTOT:
                    return
                r0 = JJ * CH
                xb = XB2L[JJ % 2]
                B.wait("sync", sV, fin_done.get(JJ - 1, 0))
                B.wait("sync", sST, entry_sST)
                p("sync", lambda e, r0=r0, xb=xb: e.dma_start(
                    out=xb, in_=src[r0:r0 + CH, :]), (sLX, 16))

            def emit_ln(JJ):
                if JJ >= NTOT:
                    return
                xb = XB2L[JJ % 2]
                B.wait("vector", sLX)
                p("vector", lambda e, xb=xb: e.reduce_sum(s1, xb, axis=AX.X), (sV, 1))
                B.wait("scalar", sLX)
                B.wait("scalar", sV, p_done.get(JJ - 1, 0))
                p("scalar", lambda e, xb=xb: e.activation(
                    LNSQ2, xb, AF.Square, accum_out=ssq), (sS, 1))
                B.wait("vector", sV)
                B.wait("vector", sS)
                p("vector", lambda e: e.tensor_scalar_mul(mean, s1, col(C_NI1024)))
                p("vector", lambda e: e.tensor_scalar_mul(var, ssq, col(C_I1024)),
                  (sV, 1))
                B.wait("vector", sV)
                p("vector", lambda e: e.tensor_tensor(sd, mean, mean, op=ALU.mult),
                  (sV, 1))
                B.wait("vector", sV)
                p("vector", lambda e: e.tensor_tensor(var, var, sd, op=ALU.subtract),
                  (sV, 1))
                B.wait("scalar", sV)
                p("scalar", lambda e: e.activation(sd, var, AF.Sqrt, bias=col(C_EPS)),
                  (sS, 1))
                B.wait("vector", sS)
                p("vector", lambda e: e.reciprocal(rstd, sd), (sV, 1))
                B.wait("vector", sV)
                B.wait("vector", sT, trans_done.get(JJ - 1, 0))
                p("vector", lambda e, xb=xb: e.tensor_scalar(
                    UB2, xb, mean, rstd, op0=ALU.add, op1=ALU.mult), (sV, 1))
                ln_done[JJ] = sV.n

            ut_evac = {}

            def emit_ut(JJ):
                if JJ >= NTOT:
                    return
                B.wait("tensor", sV, ln_done[JJ])
                B.wait("tensor", sV)
                for h in range(2):
                    pdst = (pT0 if h == 0 else pT1)
                    for i in range(4):
                        ii = h * 4 + i
                        p("tensor", lambda e, ii=ii, pdst=pdst, i=i: e.transpose(
                            pdst[:, i * 128:(i + 1) * 128],
                            UB2[:, ii * 128:(ii + 1) * 128], IDN),
                          (sT, 1) if i == 3 else None)
                    B.wait("vector", sT)
                    p("vector", lambda e, h=h, pdst=pdst: e.tensor_copy(
                        UTH[:, h * 512:(h + 1) * 512], pdst), (sV, 1))
                    B.wait("vector", sV)
                    p("vector", lambda e, h=h, pdst=pdst: e.tensor_tensor(
                        UTL[:, h * 512:(h + 1) * 512], pdst,
                        UTH[:, h * 512:(h + 1) * 512], op=ALU.subtract), (sV, 1))
                trans_done[JJ] = sT.n
                ut_evac[JJ] = sV.n

            emit_load(0)
            emit_ln(0)
            emit_ut(0)

            for JJ in range(NTOT):
                Jloc = JJ % NCH
                r0 = JJ * CH
                if Jloc == 0:
                    B.wait("vector", sSH)
                    p("vector", lambda e: e.memset(CARRY, 0.0), (sV, 1))

                ut_done = ut_evac[JJ]
                B.wait("tensor", sLW, w_ready)
                B.wait("tensor", sV, ut_done)
                for wh, wl, pd in ((WQH, WQL, pA), (WKH, WKL, pB)):
                    for h in range(2):
                        for i in range(DC):
                            mm3(pd[:, h * 512:(h + 1) * 512],
                                UTH[:, i * 128:(i + 1) * 128],
                                UTL[:, i * 128:(i + 1) * 128],
                                wh[:, i * D + h * 512:i * D + (h + 1) * 512],
                                wl[:, i * D + h * 512:i * D + (h + 1) * 512],
                                start=(i == 0), stop=(i == DC - 1),
                                inc=(sT, 1) if (i == DC - 1 and h == 1) else None)
                q_done, k_done = sT.n - 1, sT.n
                B.wait("scalar", sT, q_done)
                B.wait("scalar", sV, p_done.get(JJ - 1, 0))
                p("scalar", lambda e: e.activation(QB, pA, AF.Silu), (sS, 1))
                q_evac = sS.n
                B.wait("scalar", sT, k_done)
                B.wait("scalar", sT, oo_done.get(JJ - 1, 0))
                p("scalar", lambda e: e.activation(EB, pB, AF.Exp), (sS, 1))
                k_evac = sS.n

                B.wait("tensor", sS, q_evac)
                for h in range(2):
                    for i in range(DC):
                        mm3(pA[:, h * 512:(h + 1) * 512],
                            UTH[:, i * 128:(i + 1) * 128],
                            UTL[:, i * 128:(i + 1) * 128],
                            WVH[:, i * D + h * 512:i * D + (h + 1) * 512],
                            WVL[:, i * D + h * 512:i * D + (h + 1) * 512],
                            start=(i == 0), stop=(i == DC - 1),
                            inc=(sT, 1) if (i == DC - 1 and h == 1) else None)
                v_done = sT.n
                B.wait("vector", sT, v_done)
                B.wait("vector", sST)
                p("vector", lambda e: e.tensor_copy(VB2, pA), (sV, 1))
                v_evac = sV.n

                for i in range(DC):
                    mm3(pT1[:, 0:128],
                        UTH[:, i * 128:(i + 1) * 128],
                        UTL[:, i * 128:(i + 1) * 128],
                        WG1H[:, i * LR:(i + 1) * LR],
                        WG1L[:, i * LR:(i + 1) * LR],
                        start=(i == 0), stop=(i == DC - 1),
                        inc=(sT, 1) if i == DC - 1 else None)
                B.wait("vector", sT)
                p("vector", lambda e: e.tensor_copy(G1B, pT1[:, 0:128]), (sV, 1))
                B.wait("tensor", sV)
                p("tensor", lambda e: e.transpose(pT0[:, 0:128], G1B, IDN), (sT, 1))
                B.wait("vector", sT)
                p("vector", lambda e: e.tensor_copy(G1TH, pT0[:, 0:128]), (sV, 1))
                B.wait("vector", sV)
                p("vector", lambda e: e.tensor_tensor(
                    G1TL, pT0[:, 0:128], G1TH, op=ALU.subtract), (sV, 1))
                B.wait("tensor", sV)
                B.wait("tensor", sS, k_evac)
                for h in range(2):
                    mm3(pB[:, h * 512:(h + 1) * 512], G1TH, G1TL,
                        WG2H[:, h * 512:(h + 1) * 512],
                        WG2L[:, h * 512:(h + 1) * 512],
                        start=True, stop=True,
                        inc=(sT, 1) if h == 1 else None)
                gate_done = sT.n
                B.wait("scalar", sT, gate_done)
                p("scalar", lambda e: e.activation(GSB, pB, AF.Silu), (sS, 1))
                gate_evac = sS.n

                B.wait("tensor", sS, k_evac)
                B.wait("tensor", sV)
                for h in range(2):
                    pdst = (pT0 if h == 0 else pT1)
                    for i in range(4):
                        ii = h * 4 + i
                        p("tensor", lambda e, ii=ii, pdst=pdst, i=i: e.transpose(
                            pdst[:, i * 128:(i + 1) * 128],
                            EB[:, ii * 128:(ii + 1) * 128], IDN),
                          (sT, 1) if i == 3 else None)
                    B.wait("vector", sT)
                    p("vector", lambda e, h=h, pdst=pdst: e.tensor_copy(
                        ET2[:, h * 512:(h + 1) * 512], pdst), (sV, 1))

                B.wait("tensor", sS, gate_evac)
                B.wait("tensor", sSH)
                B.wait("tensor", sV)
                for h in range(2):
                    p("tensor", lambda e, h=h: e.matmul(
                        pB[:, h * 512:(h + 1) * 512], TRILS,
                        EB[:, h * 512:(h + 1) * 512], start=True, stop=False))
                    p("tensor", lambda e, h=h: e.matmul(
                        pB[:, h * 512:(h + 1) * 512], ONESR,
                        CARRY[0:1, h * 512:(h + 1) * 512],
                        start=False, stop=True),
                      (sT, 1) if h == 1 else None)
                a_done = sT.n
                B.wait("vector", sT, a_done)
                p("vector", lambda e: e.tensor_copy(AB2, pB), (sV, 1))
                a_evac = sV.n
                B.wait("sync", sV, a_evac)
                p("sync", lambda e: e.dma_start(out=CARRY, in_=AB2[127:128, :]),
                  (sSH, 16))
                p("vector", lambda e: e.reciprocal(PB2, AB2), (sV, 1))
                B.wait("vector", sS, q_evac)
                p("vector", lambda e: e.scalar_tensor_tensor(
                    PB2, PB2, col(C_BSCALE), QB, op0=ALU.mult, op1=ALU.mult),
                  (sV, 1))
                p_done[JJ] = sV.n

                B.wait("tensor", sV, p_done[JJ])
                B.wait("tensor", sV)
                for h in range(2):
                    pdst = (pT0 if h == 0 else pT1)
                    for i in range(4):
                        ii = h * 4 + i
                        p("tensor", lambda e, ii=ii, pdst=pdst, i=i: e.transpose(
                            pdst[:, i * 128:(i + 1) * 128],
                            PB2[:, ii * 128:(ii + 1) * 128], IDN),
                          (sT, 1) if i == 3 else None)
                    B.wait("vector", sT)
                    p("vector", lambda e, h=h, pdst=pdst: e.tensor_copy(
                        PT2[:, h * 512:(h + 1) * 512], pdst), (sV, 1))
                pt_done = sV.n

                B.wait("tensor", sV, pt_done)
                for h in range(8):
                    p("tensor", lambda e, h=h: e.matmul(
                        pB[:, h * 128:(h + 1) * 128],
                        ET2[:, h * 128:(h + 1) * 128],
                        PT2[:, h * 128:(h + 1) * 128],
                        start=True, stop=True),
                      (sT, 1) if h == 7 else None)
                sc_done = sT.n
                B.wait("vector", sT, sc_done)
                p("vector", lambda e: e.tensor_tensor(
                    SCM, pB, TRILB, op=ALU.mult), (sV, 1))
                if Jloc > 0:
                    p("vector", lambda e: e.tensor_copy(STB, pC), (sV, 1))
                st_copy = sV.n

                B.wait("tensor", sV, st_copy)
                for h in range(8):
                    p("tensor", lambda e, h=h, Jloc=Jloc: e.matmul(
                        pA[:, h * 128:(h + 1) * 128],
                        SCM[:, h * 128:(h + 1) * 128],
                        VB2[:, h * 128:(h + 1) * 128],
                        start=True, stop=(Jloc == 0)),
                      (sT, 1) if (h == 7 and Jloc == 0) else None)
                    if Jloc > 0:
                        p("tensor", lambda e, h=h: e.matmul(
                            pA[:, h * 128:(h + 1) * 128],
                            PT2[:, h * 128:(h + 1) * 128],
                            STB[:, h * 128:(h + 1) * 128],
                            start=False, stop=True),
                          (sT, 1) if h == 7 else None)
                o_done = sT.n

                for h in range(8):
                    p("tensor", lambda e, h=h, Jloc=Jloc: e.matmul(
                        pC[:, h * 128:(h + 1) * 128],
                        EB[:, h * 128:(h + 1) * 128],
                        VB2[:, h * 128:(h + 1) * 128],
                        start=(Jloc == 0 and h % 4 == 0),
                        stop=(Jloc == NCH - 1 and h % 4 == 3)),
                      (sT, 1) if h == 7 else None)
                ds_done = sT.n

                emit_load(JJ + 1)
                emit_ln(JJ + 1)

                B.wait("scalar", sT, o_done)
                B.wait("scalar", sV, p_done[JJ])
                p("scalar", lambda e: e.activation(
                    LNSQ2, pA, AF.Square, accum_out=ssq), (sS, 1))
                B.wait("scalar", sS)
                p("scalar", lambda e: e.activation(
                    sd, ssq, AF.Sqrt, bias=col(C_EPS), scale=col(C_I1024)),
                  (sS, 1))
                B.wait("vector", sS)
                p("vector", lambda e: e.reciprocal(rstd, sd), (sV, 1))
                B.wait("vector", sV)
                B.wait("vector", sS, gate_evac)
                p("vector", lambda e: e.scalar_tensor_tensor(
                    OB, pA, rstd, GSB, op0=ALU.mult, op1=ALU.mult), (sV, 1))
                ob_done = sV.n

                emit_ut(JJ + 1)

                B.wait("tensor", sV, ob_done)
                B.wait("tensor", sV, ut_evac.get(JJ + 1, 0))
                B.wait("tensor", sT, ds_done)
                for h in range(2):
                    pdst = (pT0 if h == 0 else pT1)
                    for i in range(4):
                        ii = h * 4 + i
                        p("tensor", lambda e, ii=ii, pdst=pdst, i=i: e.transpose(
                            pdst[:, i * 128:(i + 1) * 128],
                            OB[:, ii * 128:(ii + 1) * 128], IDN),
                          (sT, 1) if i == 3 else None)
                    B.wait("vector", sT)
                    p("vector", lambda e, h=h, pdst=pdst: e.tensor_copy(
                        OTH[:, h * 512:(h + 1) * 512], pdst), (sV, 1))
                    B.wait("vector", sV)
                    p("vector", lambda e, h=h, pdst=pdst: e.tensor_tensor(
                        OTL[:, h * 512:(h + 1) * 512], pdst,
                        OTH[:, h * 512:(h + 1) * 512], op=ALU.subtract), (sV, 1))
                ot_evac[JJ] = sV.n

                B.wait("tensor", sV, ot_evac[JJ])
                for h in range(2):
                    for i in range(DC):
                        mm3(pB[:, h * 512:(h + 1) * 512],
                            OTH[:, i * 128:(i + 1) * 128],
                            OTL[:, i * 128:(i + 1) * 128],
                            WOH[:, i * D + h * 512:i * D + (h + 1) * 512],
                            WOL[:, i * D + h * 512:i * D + (h + 1) * 512],
                            start=(i == 0), stop=(i == DC - 1),
                            inc=(sT, 1) if (i == DC - 1 and h == 1) else None)
                oo_done[JJ] = sT.n
                B.wait("vector", sT, oo_done[JJ])
                B.wait("vector", sST)
                p("vector", lambda e, JJ=JJ: e.tensor_tensor(
                    FOUT2, pB, XB2L[JJ % 2], op=ALU.add), (sV, 1))
                fin_done[JJ] = sV.n
                B.wait("gpsimd", sV, fin_done[JJ])
                p("gpsimd", lambda e, r0=r0, dst=dst: e.dma_start(
                    out=dst[r0:r0 + CH, :], in_=FOUT2), (sST, 16))

        chain = []
        if n_passes >= 1:
            chain.append(emit_pass_attn_a)
        if n_passes >= 2:
            chain.append(emit_pass_ffn)
        if n_passes >= 3:
            chain.append(emit_pass_attn_b)
        if n_passes >= 4:
            chain.append(emit_pass_ffn)
        srcs = [x_in] + [resid] * (len(chain) - 1)
        for i, fn in enumerate(chain):
            dst = out if i == len(chain) - 1 else resid
            fn(srcs[i], dst)

        B.wait("gpsimd", sST)

        import contextlib
        with contextlib.ExitStack() as stack:
            for s in B.sems:
                s.handle = stack.enter_context(nc.semaphore(s.name))
            for eng in ENGINES:
                stream = B.streams[eng]
                getattr(block, eng)(lambda e, stream=stream: [f(e) for f in stream])

    return nc



def _consts():
    idn = np.eye(128, dtype=np.float32)
    ccols = np.zeros((128, 16), dtype=np.float32)
    vals = {C_EPS: 1e-5, C_I1024: 1.0 / 1024.0, C_NI1024: -1.0 / 1024.0,
            C_I16: 1.0 / 16.0, C_HALF: 0.5, C_ONE: 1.0, C_NONE: -1.0,
            C_BSCALE: 1.0 / np.sqrt(128.0), C_ZERO: 0.0, C_NI16: -1.0 / 16.0,
            C_LNBS: float(-0.5 * np.log(128.0))}
    for k, v in vals.items():
        ccols[:, k] = v
    ones = np.ones((1, 128), dtype=np.float32)
    k_i = np.arange(128)[:, None]
    c_i = np.arange(128)[None, :]
    diffa = np.where(c_i > k_i, (c_i - 1 - k_i).astype(np.float64), 1e9)
    diffa = diffa.astype(np.float32)
    c_col = np.arange(128)[:, None]
    j_row = np.arange(NCH)[None, :]
    cmw = (127 - c_col - 128 * j_row).astype(np.float32)
    cmi = (c_col + 128 * (j_row - 1)).astype(np.float32)
    cmi[:, 0] = 1e9
    tril = (k_i <= c_i).astype(np.float32)
    e = np.arange(E16)
    u16 = (e[:, None] <= e[None, :]).astype(np.float32)
    e64c = np.zeros((128, 192), dtype=np.float32)
    e64c[:, 0:64] = np.tile(e, 4)[None, :]
    e64c[:, 64:128] = np.tile(1.0 / (e + 1.0), 4)[None, :]
    e64c[:, 128:192] = np.tile(e + 1.0, 4)[None, :]
    import ml_dtypes
    trilb = np.tile(tril, (1, 8)).astype(ml_dtypes.bfloat16)
    return dict(idn=idn, ccols=ccols, onesr=ones, diffa=diffa, cmw=cmw,
                cmi=cmi, tril=tril, u16=u16, e64c=e64c, trilb=trilb)


def prep_in_maps(inputs, n_passes=4):
    f32 = lambda a: np.ascontiguousarray(np.asarray(a, dtype=np.float32))
    x = f32(inputs["L0_attn_a_out"])
    wwr = f32(np.asarray(inputs["L0_attn_a_W_write"]).T)
    wrd = f32(np.asarray(inputs["L0_attn_a_W_read"]).T)
    scal = np.stack([np.float32(inputs["L0_attn_a_decay"]),
                     np.float32(inputs["L0_attn_a_log_alpha"])]).reshape(1, 2)
    scal = f32(scal)
    wq = f32(np.asarray(inputs["L0_attn_b_Wq"]).T)
    wk = f32(np.asarray(inputs["L0_attn_b_Wk"]).T)
    wv = f32(np.asarray(inputs["L0_attn_b_Wv"]).T)
    wo = f32(np.asarray(inputs["L0_attn_b_Wo"]).T)
    wg1 = f32(np.asarray(inputs["L0_attn_b_Wg1"]).T)
    wg2 = f32(np.asarray(inputs["L0_attn_b_Wg2"]).T)
    proj = f32(inputs["L0_ffn_proj"])
    c1c2 = f32(np.concatenate([np.asarray(inputs["L0_ffn_C1"]),
                               np.asarray(inputs["L0_ffn_C2"])], axis=1))
    win1_full = np.asarray(inputs["L0_ffn_Win1"])
    win1 = f32(win1_full[:D])
    w1b = f32(win1_full[D:D + 1])
    wout1 = f32(np.asarray(inputs["L0_ffn_Wout1"]).T)
    win2_full = np.asarray(inputs["L0_ffn_Win2"])
    win2 = f32(win2_full[:O])
    w2b = f32(win2_full[O:O + 1])
    wout2 = f32(np.asarray(inputs["L0_ffn_Wout2"]).T)
    cst = _consts()

    shared = dict(wwr=wwr, wrd=wrd, scal=scal, wq=wq, wk=wk, wv=wv, wo=wo,
                  wg1=wg1, wg2=wg2, proj=proj, c1c2=c1c2, win1=win1, w1b=w1b,
                  wout1=wout1, win2=win2, w2b=w2b, wout2=wout2, **cst)
    in_maps = []
    for c in range(NCORES):
        m = dict(shared)
        m["x_in"] = np.ascontiguousarray(
            x[c * NSEQ:(c + 1) * NSEQ].reshape(NSEQ * T, D))
        in_maps.append(m)
    return in_maps


def kernel(**inputs):
    in_maps = prep_in_maps(inputs)
    nc = build_program(n_passes=4)
    res = run_bass_kernel_spmd(nc, in_maps, core_ids=list(range(NCORES)))
    parts = [res.results[c]["out"].reshape(NSEQ, T, D) for c in range(NCORES)]
    return np.concatenate(parts, axis=0)
